# version 21
# baseline (speedup 1.0000x reference)
import contextlib
import numpy as np
import ml_dtypes
import concourse.bass as bass
import concourse.mybir as mybir
from concourse.alu_op_type import AluOpType as OP
from concourse.bass_utils import run_bass_kernel_spmd

F32 = mybir.dt.float32
BF16 = mybir.dt.bfloat16
AF = mybir.ActivationFunctionType
AX = mybir.AxisListType

NB = 4
L = 2048
D = 1024
NDC = 8
NSEQC = 16
NQT = 4
H = 16
NHP = 8
NS = 4096
NSC = 32
HIDC = 32
BND_SCALE = -0.126


class Sem:
    def __init__(self, h):
        self.h = h
        self.n = 0

    def bump(self, k=1):
        self.n += k
        return self.n


class Script:
    ENGS = ("sync", "tensor", "vector", "scalar", "gpsimd")

    def __init__(self, nc):
        self.nc = nc
        self.q = {e: [] for e in self.ENGS}
        self._waited = {e: {} for e in self.ENGS}

    def _wait(self, eng, sem, val):
        if val is None or val <= 0:
            return
        w = self._waited[eng]
        if w.get(sem, -1) >= val:
            return
        w[sem] = val

        def f(e, h=sem.h, v=val):
            e.wait_ge(h, v)
        self.q[eng].append(f)

    def op(self, eng, method, *args, waits=(), sem=None, k=1, ser=False, **kw):
        if ser:
            own = self.eng2sem.get(eng)
            if own is not None:
                self._wait(eng, own, own.n)
        for wspec in waits:
            if isinstance(wspec, tuple):
                s, v = wspec
            else:
                s, v = wspec, wspec.n
            self._wait(eng, s, v)
        out_n = None
        if sem is not None:
            out_n = sem.bump(k)

        def f(e, method=method, args=args, kw=kw, sem=sem, k=k):
            r = getattr(e, method)(*args, **kw)
            if sem is not None:
                r.then_inc(sem.h, k)
        self.q[eng].append(f)
        return out_n

    def wait(self, eng, sem, val=None):
        self._wait(eng, sem, sem.n if val is None else val)

    def emit(self, name=None):
        with self.nc.Block(name) as blk:
            for eng in self.ENGS:
                fns = self.q[eng]
                if not fns:
                    continue
                getattr(blk, eng)(lambda e, fns=fns: [f(e) for f in fns])
        self.q = {e: [] for e in self.ENGS}
        self._waited = {e: {} for e in self.ENGS}


class Ctx:
    pass


def build_program(nb=NB, debug=False):
    nc = bass.Bass("TRN2", target_bir_lowering=False)
    g = Ctx()
    g.nc = nc
    g.nb = nb

    g.x_in = nc.dram_tensor("x_in", [nb, L, D], F32, kind="ExternalInput").ap()
    g.w_qkv = nc.dram_tensor("w_qkv", [D, 3 * D], F32, kind="ExternalInput").ap()
    g.w_out = nc.dram_tensor("w_out", [D, D], F32, kind="ExternalInput").ap()
    g.w1 = nc.dram_tensor("w1", [D, 4 * D], F32, kind="ExternalInput").ap()
    g.w2 = nc.dram_tensor("w2", [4 * D, D], F32, kind="ExternalInput").ap()
    g.sl_emb = nc.dram_tensor("sl_emb", [NS, D], F32, kind="ExternalInput").ap()
    g.gx_row = nc.dram_tensor("gx_row", [1, D], F32, kind="ExternalInput").ap()
    g.gs_t = nc.dram_tensor("gs_t", [128, 8], F32, kind="ExternalInput").ap()
    g.c_identf = nc.dram_tensor("c_identf", [128, 128], F32, kind="ExternalInput").ap()
    g.c_identb = nc.dram_tensor("c_identb", [128, 128], BF16, kind="ExternalInput").ap()
    g.c_trib = nc.dram_tensor("c_trib", [128, 128], BF16, kind="ExternalInput").ap()
    g.y_out = nc.dram_tensor("y_out", [nb, L, D], F32, kind="ExternalOutput").ap()

    dbg = "ExternalOutput" if debug else "Internal"
    g.se_hbm = nc.dram_tensor("se_hbm", [NSC, 128, 1024], BF16, kind=dbg).ap()
    g.z_hbm = nc.dram_tensor("z_hbm", [NSC, 128, L], F32, kind=dbg).ap()
    g.x2_hbm = nc.dram_tensor("x2_hbm", [NSEQC, 128, D], F32, kind=dbg).ap()
    g.sl_hbm = nc.dram_tensor("sl_hbm", [8, 128, 4096], BF16, kind=dbg).ap()
    g.p_hbm = nc.dram_tensor("p_hbm", [16, 128, 2048], BF16, kind=dbg).ap()

    with contextlib.ExitStack() as es:
        E = es.enter_context

        g.pe = Sem(E(nc.semaphore("pe")))
        g.act = Sem(E(nc.semaphore("act")))
        g.dve = Sem(E(nc.semaphore("dve")))
        g.gpe = Sem(E(nc.semaphore("gpe")))
        names_sync = ["cst", "seld0", "seld1", "sest0", "sest1", "x0", "x1",
                      "x2sp", "seb0", "seb1", "zsp0", "zsp1", "slsp0", "slsp1",
                      "slb0", "slb1", "psp0", "psp1", "pld", "pld1", "pld2", "pld3", "zs0", "zs1",
                      "zs2", "zs3", "x2l0", "x2l1", "yst0", "yst1"]
        names_gp = ["wq", "wk", "wv", "wo", "w1g", "w2g",
                    "w1a", "w1b", "w1c", "w1d", "w2a", "w2b", "w2c", "w2d"]
        g.S = {n2: Sem(E(nc.semaphore(n2))) for n2 in names_sync + names_gp}

        g.identf = E(nc.sbuf_tensor("identf", [128, 128], F32))
        g.identb = E(nc.sbuf_tensor("identb", [128, 128], BF16))
        g.trib = E(nc.sbuf_tensor("trib", [128, 128], BF16))
        g.ones_row_f = E(nc.sbuf_tensor("ones_row_f", [64, 128], F32))
        g.ones_col_b = E(nc.sbuf_tensor("ones_col_b", [128, 1], BF16))
        g.ones_row_b = E(nc.sbuf_tensor("ones_row_b", [64, 128], BF16))
        g.gammabc = E(nc.sbuf_tensor("gammabc", [128, D], F32))
        g.gst = E(nc.sbuf_tensor("gst", [128, 8], F32))
        g.junk = E(nc.sbuf_tensor("junk", [128, 1024], BF16))
        g.epsc = E(nc.sbuf_tensor("epsc", [128, 2], F32))
        g.rows = E(nc.sbuf_tensor("rows", [128, 64], F32))
        g.trow = E(nc.sbuf_tensor("trow", [64, 1024], F32))
        g.bias_col = E(nc.sbuf_tensor("bias_col", [128, 1], F32))

        g.PS_A = E(nc.psum_tensor("PS_A", [128, 2048], F32))
        g.PS_B = E(nc.psum_tensor("PS_B", [128, 1024], F32))

        sc = Script(nc)
        sc.eng2sem = {"tensor": g.pe, "vector": g.dve, "scalar": g.act,
                      "gpsimd": g.gpe}
        g.sc = sc

        _s0(g, es)

        for b in range(nb):
            with contextlib.ExitStack() as es_b:
                _batch_attn_p1(g, es_b, b)
                with contextlib.ExitStack() as es_p2:
                    _ffn_p2(g, es_p2, b)
                with contextlib.ExitStack() as es_p3:
                    _ffn_p3(g, es_p3, b)

        sc.wait("sync", g.S["yst0"])
        sc.wait("sync", g.S["yst1"])
        sc.emit("fin")

    return nc


def _s0(g, es):
    nc, sc = g.nc, g.sc
    pe, act, dve, gpe = g.pe, g.act, g.dve, g.gpe
    cst = g.S["cst"]

    sc.op("sync", "dma_start", g.identf[:], g.c_identf, sem=cst, k=16)
    sc.op("sync", "dma_start", g.identb[:], g.c_identb, sem=cst, k=16)
    sc.op("sync", "dma_start", g.trib[:], g.c_trib, sem=cst, k=16)
    sc.op("sync", "dma_start", g.gst[:], g.gs_t, sem=cst, k=16)
    sc.op("gpsimd", "memset", g.ones_row_f[:], 1.0, sem=gpe)
    sc.op("gpsimd", "memset", g.ones_col_b[:], 1.0, sem=gpe)
    sc.op("gpsimd", "memset", g.ones_row_b[:], 1.0, sem=gpe)
    sc.op("gpsimd", "memset", g.epsc[:, 0:1], 1e-30, sem=gpe)
    sc.op("gpsimd", "memset", g.epsc[:, 1:2], 1e-5, sem=gpe)
    with contextlib.ExitStack() as es00:
        gxr = es00.enter_context(nc.sbuf_tensor("gxr", [1, D], F32))
        sc.op("sync", "dma_start", gxr[:], g.gx_row, sem=cst, k=16)
        t_gx = cst.n
        for i in range(2):
            sc.op("tensor", "matmul", g.PS_B[:, 512 * i:512 * (i + 1)],
                  g.ones_row_f[0:1, :], gxr[0:1, 512 * i:512 * (i + 1)],
                  start=True, stop=True,
                  waits=((cst, t_gx), (gpe, 4)), sem=pe)
        sc.op("scalar", "copy", g.gammabc[:], g.PS_B[:, 0:1024], waits=(pe,), sem=act)
        sc.emit("s0_const")

    with contextlib.ExitStack() as es0:
        E0 = es0.enter_context
        PS_X = E0(nc.psum_tensor("PS_X0", [128, 1024], BF16))
        srows = E0(nc.sbuf_tensor("srows", [128, 16], F32))
        se_tm = E0(nc.sbuf_tensor("se_tm", [128, 2 * D], F32))
        se_bf = E0(nc.sbuf_tensor("se_bf", [128, 2 * D], BF16))
        se_st = E0(nc.sbuf_tensor("se_st", [128, 2 * D], BF16))
        seld = [g.S["seld0"], g.S["seld1"]]
        sest = [g.S["sest0"], g.S["sest1"]]
        t_rd = [0, 0]
        t_tr = [0, 0]
        t_bf_ok = [0, 0]
        t_rd_act = [0, 0]
        t_ps = [0]

        def s0_tail(c):
            r = c % 2
            bf = se_bf[:, r * D:(r + 1) * D]
            st = se_st[:, r * D:(r + 1) * D]
            for dc in range(8):
                sc.op("tensor", "transpose", PS_X[:, 128 * dc:128 * (dc + 1)],
                      bf[:, 128 * dc:128 * (dc + 1)], g.identb[:],
                      waits=((dve, t_bf_ok[r]), (act, t_ps[0])), sem=pe)
            t_tr[r] = pe.n
            for dc in range(8):
                sc.op("scalar", "activation", st[:, 128 * dc:128 * (dc + 1)],
                      PS_X[:, 128 * dc:128 * (dc + 1)], AF.Copy,
                      scale=g.gst[:, dc:dc + 1],
                      waits=((pe, t_tr[r]), (cst, t_gx), (sest[r], sest[r].n)),
                      sem=act)
            t_ps[0] = act.n
            sc.op("sync", "dma_start", g.se_hbm[c], st,
                  waits=(act,), sem=sest[r], k=16)

        for c in range(NSC):
            r = c % 2
            tm = se_tm[:, r * D:(r + 1) * D]
            bf = se_bf[:, r * D:(r + 1) * D]
            sc.op("sync", "dma_start", tm, g.sl_emb[128 * c:128 * (c + 1), :],
                  waits=((dve, t_rd[r]), (act, t_rd_act[r])), sem=seld[r], k=16)
            if c > 0:
                s0_tail(c - 1)
            RS = lambda i: srows[:, 8 * r + i:8 * r + i + 1]
            sc.op("scalar", "activation", g.junk[:, 0:D], tm, AF.Square,
                  accum_out=RS(0), waits=(seld[r],), sem=act)
            t_rd_act[r] = act.n
            sc.op("scalar", "activation", RS(1), RS(0), AF.Ln,
                  bias=g.epsc[:, 0:1], sem=act, ser=True)
            sc.op("scalar", "activation", RS(2), RS(1), AF.Exp,
                  scale=0.5, sem=act, ser=True)
            sc.op("vector", "tensor_scalar", RS(3), RS(2),
                  1e-12, None, OP.max, waits=(act,), sem=dve)
            sc.op("vector", "reciprocal", RS(4), RS(3), sem=dve,
                  ser=True)
            sc.op("vector", "tensor_scalar", RS(5), RS(4),
                  32.0, None, OP.mult, sem=dve, ser=True)
            sc.op("vector", "tensor_scalar", bf, tm, RS(5), None,
                  OP.mult, waits=((pe, t_tr[r]),), sem=dve, ser=True)
            t_bf_ok[r] = dve.n
            t_rd[r] = dve.n
        s0_tail(NSC - 1)
        for s2 in (seld[0], seld[1], sest[0], sest[1]):
            sc.wait("sync", s2)
        sc.emit("s0_se")


def _batch_attn_p1(g, es_b, b):
    nc, sc = g.nc, g.sc
    pe, act, dve, gpe = g.pe, g.act, g.dve, g.gpe
    PS_A, PS_B = g.PS_A, g.PS_B
    sx = [g.S["x0"], g.S["x1"]]
    es_qkv = contextlib.ExitStack()
    EB = es_qkv.enter_context
    PS_C = EB(nc.psum_tensor(f"PS_Ca_{b}", [128, 512], F32))

    q_fm = EB(nc.sbuf_tensor(f"q_fm_{b}", [128, NHP * L], BF16))
    k_fm = EB(nc.sbuf_tensor(f"k_fm_{b}", [128, NHP * L], BF16))
    v_tm = EB(nc.sbuf_tensor(f"v_tm_{b}", [128, NSEQC * 1040], BF16))
    x_tm = EB(nc.sbuf_tensor(f"x_tm_{b}", [128, 2 * D], F32))

    qf = lambda hp, s0, n: q_fm[:, hp * L + s0: hp * L + s0 + n]
    kf = lambda hp, s0, n: k_fm[:, hp * L + s0: hp * L + s0 + n]
    vt = lambda scn, c0, n: v_tm[:, scn * 1040 + c0: scn * 1040 + c0 + n]

    for eng in ("gpsimd", "sync"):
        sc.wait(eng, pe)
        sc.wait(eng, act)
        sc.wait(eng, dve)
    g.b_m = es_b.enter_context(
        nc.sbuf_tensor(f"b_m_{b}", [128, L], F32, side="right"))
    es_xn = contextlib.ExitStack()
    for scn in range(NSEQC):
        sc.op("gpsimd", "memset",
              v_tm.rearrange("p (s h c) -> p s h c", s=NSEQC, h=16)[:, scn, :, 64:65],
              1.0, sem=gpe)
    t_vones = gpe.n

    with contextlib.ExitStack() as es_a1:
        EA = es_a1.enter_context
        PS_Cb = EA(nc.psum_tensor(f"PS_Cb_{b}", [128, 512], F32))
        x_bf = EA(nc.sbuf_tensor(f"x_bf_{b}", [128, NDC * L], BF16))
        wbuf = EA(nc.sbuf_tensor(f"wbuf_{b}", [128, 2 * NDC * 1024], BF16))
        ksq = EA(nc.sbuf_tensor(f"ksq_{b}", [128, 2 * L], BF16))
        wv = lambda r, dc: wbuf[:, (r * NDC + dc) * 1024:(r * NDC + dc + 1) * 1024]
        xb = lambda dc, s0, n: x_bf[:, dc * L + s0: dc * L + s0 + n]

        t_free = [0, 0]
        t_psA = [0, 0]

        def x_load(s2):
            r2 = s2 % 2
            sc.op("sync", "dma_start", x_tm[:, r2 * D:(r2 + 1) * D],
                  g.x_in[b, 128 * s2:128 * (s2 + 1), :],
                  waits=((pe, t_free[r2]),), sem=sx[r2], k=16)

        x_load(0)
        for scn in range(NSEQC):
            r = scn % 2
            xt = x_tm[:, r * D:(r + 1) * D]
            if scn + 1 < NSEQC:
                x_load(scn + 1)
            psx = PS_A[:, r * 1024:(r + 1) * 1024]
            for dc in range(NDC):
                sc.op("tensor", "transpose", psx[:, 128 * dc:128 * (dc + 1)],
                      xt[:, 128 * dc:128 * (dc + 1)], g.identf[:],
                      waits=((sx[r], sx[r].n), (dve, t_psA[r])), sem=pe)
            t_free[r] = pe.n
            sc.op("vector", "tensor_copy",
                  x_bf.rearrange("p (dc s) -> p dc s", dc=NDC)[
                      :, :, 128 * scn:128 * (scn + 1)],
                  psx.rearrange("p (dc c) -> p dc c", dc=NDC),
                  waits=(pe,), sem=dve)
            t_psA[r] = dve.n

        swq, swk, swv = g.S["wq"], g.S["wk"], g.S["wv"]
        for dc in range(NDC):
            sc.op("gpsimd", "dma_start", wv(0, dc),
                  g.w_qkv[128 * dc:128 * (dc + 1), 0:1024], sem=swq, k=16)
        for dc in range(NDC):
            sc.op("gpsimd", "dma_start", wv(1, dc),
                  g.w_qkv[128 * dc:128 * (dc + 1), 1024:2048], sem=swk, k=16)

        t_ev = [0, 0]
        t_xbf = dve.n
        for (name, ws, wr, dst) in (("q", swq, 0, qf), ("k", swk, 1, kf)):
            for hp in range(NHP):
                for st in range(4):
                    r = (hp * 4 + st) % 2
                    ps = PS_B[:, 512 * r:512 * (r + 1)]
                    for dc in range(NDC):
                        w = ((ws, ws.n), (act, t_ev[r]), (dve, t_xbf))                             if dc == 0 else ()
                        sc.op("tensor", "matmul", ps,
                              wv(wr, dc)[:, 128 * hp:128 * (hp + 1)],
                              xb(dc, 512 * st, 512),
                              start=(dc == 0), stop=(dc == NDC - 1),
                              waits=w, sem=pe)
                    sc.op("scalar", "copy", dst(hp, 512 * st, 512), ps,
                          waits=(pe,), sem=act)
                    t_ev[r] = act.n
            if name == "q":
                sc.wait("gpsimd", pe)
                for dc in range(NDC):
                    sc.op("gpsimd", "dma_start", wv(0, dc),
                          g.w_qkv[128 * dc:128 * (dc + 1), 2048:3072],
                          sem=swv, k=16)

        t_evv = [0, 0]
        for scn in range(NSEQC):
            for vf in range(2):
                r = (scn * 2 + vf) % 2
                ps = PS_B[:, 512 * r:512 * (r + 1)]
                for dc in range(NDC):
                    w = [(swv, swv.n)]
                    if dc == 0:
                        w += [(dve, t_evv[r]), (act, t_ev[r]), (gpe, t_vones)]
                    sc.op("tensor", "matmul", ps,
                          xb(dc, 128 * scn, 128),
                          wv(0, dc)[:, 512 * vf:512 * (vf + 1)],
                          start=(dc == 0), stop=(dc == NDC - 1),
                          waits=w if dc == 0 else (), sem=pe)
                sc.op("vector", "tensor_copy",
                      v_tm.rearrange("p (s h c) -> p s h c", s=NSEQC, h=16)[
                          :, scn, vf * 8:(vf + 1) * 8, 0:64],
                      ps.rearrange("p (h c) -> p h c", h=8),
                      waits=(pe,), sem=dve)
                t_evv[r] = dve.n

        t_ksq2 = [0, 0]
        t_psn_rd = [0, 0]
        idx = 0
        for ti, src in enumerate((qf, kf)):
            first = True
            rr = 0
            for hp in range(NHP):
                kk = ksq[:, (hp % 2) * L:(hp % 2 + 1) * L]
                sc.op("vector", "tensor_tensor", kk, src(hp, 0, L),
                      src(hp, 0, L), OP.mult,
                      waits=((pe, t_ksq2[hp % 2]), (act, t_ev[0]),
                             (act, t_ev[1])),
                      sem=dve)
                t_ksq_ok = dve.n
                for st in range(4):
                    idx2 = (hp * 4 + st) % 2
                    psn = PS_C if idx2 == 0 else PS_Cb
                    sc.op("tensor", "matmul", psn[0:1, 0:512],
                          g.ones_col_b[:], kk[:, 512 * st:512 * (st + 1)],
                          start=True, stop=True,
                          waits=((dve, t_ksq_ok), (dve, t_psn_rd[idx2])),
                          sem=pe)
                    t_ksq2[hp % 2] = pe.n
                    nrr = 1 - rr
                    if first:
                        sc.op("vector", "tensor_copy",
                              g.trow[32 * nrr:32 * nrr + 1, 0:512],
                              psn[0:1, 0:512], waits=(pe,), sem=dve, ser=True)
                        first = False
                    else:
                        sc.op("vector", "tensor_tensor",
                              g.trow[32 * nrr:32 * nrr + 1, 0:512],
                              psn[0:1, 0:512], g.trow[32 * rr:32 * rr + 1, 0:512],
                              OP.max, waits=(pe,), sem=dve, ser=True)
                    t_psn_rd[idx2] = dve.n
                    rr = nrr
            sc.op("vector", "tensor_reduce", g.rows[0:1, 8 + ti:9 + ti],
                  g.trow[32 * rr:32 * rr + 1, 0:512], AX.X, OP.max, sem=dve,
                  ser=True)
        sc.op("vector", "tensor_tensor", g.rows[0:1, 10:11], g.rows[0:1, 8:9],
              g.rows[0:1, 9:10], OP.mult, sem=dve, ser=True)
        sc.op("scalar", "activation", g.rows[0:1, 11:12], g.rows[0:1, 10:11],
              AF.Ln, bias=g.epsc[0:1, 0:1], waits=(dve,), sem=act)
        sc.op("scalar", "activation", g.rows[0:1, 12:13], g.rows[0:1, 11:12],
              AF.Exp, scale=0.5, sem=act, ser=True)
        sc.op("vector", "tensor_scalar", g.rows[0:1, 13:14], g.rows[0:1, 12:13],
              BND_SCALE, None, OP.mult, waits=(act,), sem=dve)
        sc.op("tensor", "matmul", PS_C[:, 0:1], g.ones_row_f[0:1, :],
              g.rows[0:1, 13:14], start=True, stop=True, waits=(dve,), sem=pe)
        sc.op("vector", "tensor_copy", g.bias_col[:], PS_C[:, 0:1],
              waits=(pe,), sem=dve)
        t_bias = dve.n
        sc.emit(f"b{b}_a15")

    with contextlib.ExitStack() as es_a2:
        EA = es_a2.enter_context
        PS_X = EA(nc.psum_tensor(f"PS_Xa_{b}", [128, 1024], BF16))
        wout = EA(nc.sbuf_tensor(f"wout_{b}", [128, NDC * 1024], BF16))
        attnout = EA(nc.sbuf_tensor(f"attnout_{b}", [128, NHP * 512], BF16))
        ebuf = EA(nc.sbuf_tensor(f"ebuf_{b}", [128, 2 * 2048], BF16))
        proj_fm = EA(nc.sbuf_tensor(f"proj_fm_{b}", [128, NDC * 512], BF16))
        x1b = EA(nc.sbuf_tensor(f"x1b_{b}", [128, D], F32))
        x2b = EA(nc.sbuf_tensor(f"x2b_{b}", [128, D], F32))
        bcnt = EA(nc.sbuf_tensor(f"bcnt_{b}", [128, 512], BF16))
        bcn = bcnt[:, :]
        avb = EA(nc.sbuf_tensor(f"avb_{b}", [128, 2 * 512], BF16))
        xn_tm = es_xn.enter_context(
            nc.sbuf_tensor(f"xn_tm_{b}", [128, NSEQC * D], BF16, side="right"))
        g.xn_tm = xn_tm

        swo = g.S["wo"]
        for fc in range(NDC):
            sc.op("gpsimd", "dma_start", wout[:, fc * 1024:(fc + 1) * 1024],
                  g.w_out[128 * fc:128 * (fc + 1), :], sem=swo, k=16)
        t_wo = swo.n

        ps_s2 = [PS_A[:, 0:1024], PS_A[:, 1024:2048]]
        ps_av2 = [PS_B[:, 0:512], PS_B[:, 512:1024]]
        t_exp = [0, 0]
        t_av_rd = [0, 0]
        t_avev = [0, 0]
        t_e_rd = {}
        t_bcn_rd = [0, 0]
        t_bcn_dve = [0, 0]
        t_projA = 0
        t_projB = 0
        t_psx_rd = 0
        t_xres = [0, 0]
        t_at_rd = 0

        def eslot(ha, gi):
            i = (gi % 2) * 2 + ha
            return ebuf[:, i * 1024:(i + 1) * 1024]

        def emit_scores(qt, hp, ha, gi):
            rbase = 64 * ha
            for ci in range(2):
                kc = 2 * gi + ci
                off = max(0, 128 * kc - 512 * qt)
                sc.op("tensor", "matmul",
                      ps_s2[ha][:, 512 * ci + off:512 * (ci + 1)],
                      kf(hp, 128 * kc, 128)[rbase:rbase + 64, :],
                      qf(hp, 512 * qt + off, 512 - off)[rbase:rbase + 64, :],
                      start=True, stop=True,
                      waits=((act, t_exp[ha]),) if ci == 0 else (), sem=pe)

        def emit_exp(qt, hp, ha, gi, t_s):
            eb = eslot(ha, gi)
            diag = (128 * (2 * gi) >= 512 * qt)
            ew = ((pe, t_s), (pe, t_e_rd.get((ha, gi % 2), 0)), (dve, t_bias))
            if not diag:
                sc.op("scalar", "activation", eb[:], ps_s2[ha][:],
                      AF.Exp, bias=g.bias_col[:, 0:1], scale=0.125,
                      waits=ew, sem=act)
            else:
                for ci in range(2):
                    kc = 2 * gi + ci
                    off = max(0, 128 * kc - 512 * qt)
                    sc.op("scalar", "activation",
                          eb[:, 512 * ci + off:512 * (ci + 1)],
                          ps_s2[ha][:, 512 * ci + off:512 * (ci + 1)],
                          AF.Exp, bias=g.bias_col[:, 0:1], scale=0.125,
                          waits=ew, sem=act)
                for ci in range(2):
                    kc = 2 * gi + ci
                    off = max(0, 128 * kc - 512 * qt)
                    sc.op("vector", "tensor_tensor",
                          eb[:, 512 * ci + off:512 * ci + off + 128],
                          eb[:, 512 * ci + off:512 * ci + off + 128],
                          g.trib[:], OP.mult, waits=(act,), sem=dve)
            t_exp[ha] = act.n

        def emit_av(qt, hp, ha, gi, G):
            hg = 2 * hp + ha
            eb = eslot(ha, gi)
            diag = (128 * (2 * gi) >= 512 * qt)
            for ci in range(2):
                kc = 2 * gi + ci
                off = max(0, 128 * kc - 512 * qt)
                w = [(act, t_exp[ha])]
                if diag:
                    w.append((dve, dve.n))
                if gi == 0 and ci == 0:
                    w += [(act, t_avev[ha]),
                          (act, t_projA if ha == 0 else t_projB)]
                sc.op("tensor", "matmul",
                      ps_av2[ha][0:65, off:512],
                      vt(kc, 65 * hg, 65),
                      eb[:, 512 * ci + off:512 * (ci + 1)],
                      start=(gi == 0 and ci == 0),
                      stop=(gi == G - 1 and ci == 1),
                      waits=w, sem=pe)
            t_e_rd[(ha, gi % 2)] = pe.n

        pending = [None]

        def flush_finale():
            if pending[0] is None:
                return
            hp0, t_ln = pending[0]
            pending[0] = None
            slot = (hp0 % 2) * 512
            sl2 = avb[:, slot:slot + 512]
            for ha in range(2):
                sc.op("tensor", "matmul",
                      PS_C[64 * ha:64 * ha + 64, 0:512],
                      g.ones_row_b[32 * ha:32 * ha + 1, 0:64],
                      g.junk[32 * ha:32 * ha + 1,
                             512 * (hp0 % 2):512 * (hp0 % 2) + 512],
                      start=True, stop=True,
                      waits=((act, t_ln), (dve, t_bcn_dve[0]))
                      if ha == 0 else (), sem=pe)
            t_bcn_rd[0] = pe.n
            sc.op("vector", "tensor_copy", bcn, PS_C[0:128, 0:512],
                  waits=(pe,), sem=dve, ser=True)
            for ha in range(2):
                rbase = 64 * ha
                sc.op("vector", "tensor_tensor",
                      attnout[rbase:rbase + 64, hp0 * 512:(hp0 + 1) * 512],
                      sl2[rbase:rbase + 64, :], bcn[rbase:rbase + 64, :],
                      OP.mult, waits=((pe, t_at_rd),), sem=dve, ser=True)
                t_av_rd[ha] = dve.n
            t_bcn_dve[0] = dve.n

        for qt in range(NQT):
            G = 2 * (qt + 1)
            for hp in range(NHP):
                for gi in range(G):
                    emit_scores(qt, hp, 0, gi)
                    emit_scores(qt, hp, 1, gi)
                    t_s = pe.n
                    if gi == 0:
                        flush_finale()
                    if gi > 0:
                        emit_av(qt, hp, 0, gi - 1, G)
                        emit_av(qt, hp, 1, gi - 1, G)
                    diag = (128 * (2 * gi) >= 512 * qt)
                    if not diag:
                        par = gi % 2
                        eb2 = ebuf[:, par * 2048:(par + 1) * 2048]
                        ew = ((pe, t_s), (pe, t_e_rd.get((0, par), 0)),
                              (pe, t_e_rd.get((1, par), 0)), (dve, t_bias))
                        sc.op("scalar", "activation", eb2[:], PS_A[:],
                              AF.Exp, bias=g.bias_col[:, 0:1], scale=0.125,
                              waits=ew, sem=act)
                        t_exp[0] = act.n
                        t_exp[1] = act.n
                    else:
                        emit_exp(qt, hp, 0, gi, t_s)
                        emit_exp(qt, hp, 1, gi, t_s)
                emit_av(qt, hp, 0, G - 1, G)
                emit_av(qt, hp, 1, G - 1, G)
                slot = (hp % 2) * 512
                sl2 = avb[:, slot:slot + 512]
                for ha in range(2):
                    sc.op("vector", "tensor_copy", sl2[64 * ha:64 * ha + 64, :],
                          ps_av2[ha][0:64, 0:512],
                          waits=((pe, pe.n), (dve, t_av_rd[ha])), sem=dve)
                    sc.op("vector", "tensor_copy",
                          g.trow[32 * ha:32 * ha + 1,
                                 512 * (hp % 2):512 * (hp % 2) + 512],
                          ps_av2[ha][64:65, 0:512],
                          waits=((pe, t_bcn_rd[0]),), sem=dve)
                    t_avev[ha] = dve.n
                for ha in range(2):
                    rrow = g.trow[32 * ha:32 * ha + 1,
                                  512 * (hp % 2):512 * (hp % 2) + 512]
                    sc.op("scalar", "activation", rrow, rrow, AF.Ln,
                          waits=((dve, t_avev[1]), (pe, t_bcn_rd[0])), sem=act)
                    sc.op("scalar", "activation",
                          g.junk[32 * ha:32 * ha + 1,
                                 512 * (hp % 2):512 * (hp % 2) + 512],
                          rrow, AF.Exp, scale=-1.0, sem=act, ser=True)
                pending[0] = (hp, act.n)
            flush_finale()
            t_heads = dve.n

            for dc in range(NDC):
                pr = dc % 2
                ps_p = PS_B[:, 512 * pr:512 * (pr + 1)]
                w = [(swo, t_wo), (dve, t_heads),
                     (act, t_projA if pr == 0 else t_projB)]
                for fc in range(NDC):
                    sc.op("tensor", "matmul", ps_p,
                          wout[:, fc * 1024 + 128 * dc:fc * 1024 + 128 * (dc + 1)],
                          attnout[:, fc * 512:(fc + 1) * 512],
                          start=(fc == 0), stop=(fc == NDC - 1),
                          waits=w if fc == 0 else (), sem=pe)
                t_at_rd = pe.n
                sc.op("scalar", "copy", proj_fm[:, dc * 512:(dc + 1) * 512], ps_p,
                      waits=(pe,), sem=act)
                if pr == 0:
                    t_projA = act.n
                else:
                    t_projB = act.n

            def xres_load(scl2):
                scn2 = 4 * qt + scl2
                r2 = scn2 % 2
                sc.op("sync", "dma_start", x_tm[:, r2 * D:(r2 + 1) * D],
                      g.x_in[b, 128 * scn2:128 * (scn2 + 1), :],
                      waits=((dve, t_xres[r2]),), sem=sx[r2], k=16)

            xres_load(0)
            xres_load(1)
            t_x1_free = getattr(g, "t_x1_free", 0)
            for scl in range(4):
                scn = 4 * qt + scl
                r = scn % 2
                xr = x_tm[:, r * D:(r + 1) * D]
                for dc in range(NDC):
                    sc.op("tensor", "transpose", PS_X[:, 128 * dc:128 * (dc + 1)],
                          proj_fm[:, dc * 512 + 128 * scl:dc * 512 + 128 * (scl + 1)],
                          g.identb[:],
                          waits=((act, max(t_projA, t_projB)), (dve, t_psx_rd))
                          if dc == 0 else (), sem=pe)
                sc.op("vector", "tensor_tensor", x1b[:], xr, PS_X[:],
                      OP.add, waits=((pe, pe.n), (sx[r], sx[r].n),
                                     (dve, t_x1_free)), sem=dve)
                t_psx_rd = dve.n
                t_xres[r] = dve.n
                if scl < 2:
                    xres_load(scl + 2)
                sx2 = g.S["x2sp"]
                _ln_rms(g, x1b, x2b[:],
                        xn_dst=xn_tm[:, scn * D:(scn + 1) * D],
                        waits_x2=((sx2, sx2.n),))
                sc.op("sync", "dma_start", g.x2_hbm[scn],
                      x2b[:], waits=(dve,), sem=sx2, k=16)
                t_x1_free = g.t_x1_free
        for s2 in (sx[0], sx[1], g.S["x2sp"]):
            sc.wait("sync", s2)
        for s2 in (swq, swk, swv, swo):
            sc.wait("gpsimd", s2)
        sc.emit(f"b{b}_a2")
    es_qkv.close()

    _ffn_p1(g, es_b, b)
    es_xn.close()


def _ln_rms(g, x1, x2_dst, xn_dst, waits_x2=()):
    sc, dve, act = g.sc, g.dve, g.act
    R = lambda i: g.rows[:, 16 + i:17 + i]
    sc.op("vector", "bn_stats", g.rows[:, 16:22], x1[:, 0:512], sem=dve, ser=True)
    sc.op("vector", "bn_stats", g.rows[:, 22:28], x1[:, 512:1024], sem=dve,
          ser=True)
    sc.op("vector", "bn_aggr", g.rows[:, 28:30], g.rows[:, 16:28], sem=dve,
          ser=True)
    mu, var = g.rows[:, 28:29], g.rows[:, 29:30]
    R = lambda i: g.rows[:, 30 + i:31 + i]
    sc.op("scalar", "activation", R(6), var, AF.Ln, bias=g.epsc[:, 1:2],
          waits=(dve,), sem=act)
    sc.op("scalar", "activation", R(7), R(6), AF.Exp, scale=-0.5, sem=act,
          ser=True)
    sc.op("scalar", "activation", R(8), var, AF.Ln, bias=g.epsc[:, 0:1], sem=act)
    sc.op("scalar", "activation", R(9), R(8), AF.Exp, scale=0.5, sem=act, ser=True)
    sc.op("vector", "tensor_scalar", x2_dst, x1[:], mu, R(7),
          OP.subtract, OP.mult, waits=((act, act.n),) + tuple(waits_x2), sem=dve,
          ser=True)
    g.t_x1_free = sc.eng2sem["vector"].n
    sc.op("vector", "tensor_tensor", R(10), R(9), R(7), OP.mult, sem=dve, ser=True)
    sc.op("vector", "tensor_scalar", R(11), R(10), 32.0, None, OP.mult, sem=dve,
          ser=True)
    sc.op("vector", "tensor_scalar", R(12), R(11), 1e-12, None, OP.max, sem=dve,
          ser=True)
    sc.op("vector", "reciprocal", R(13), R(12), sem=dve, ser=True)
    sc.op("vector", "tensor_scalar", R(14), R(13), 32.0, None, OP.mult, sem=dve,
          ser=True)
    sc.op("vector", "tensor_scalar", xn_dst, x2_dst, R(14), None, OP.mult,
          sem=dve, ser=True)
    sc.op("vector", "tensor_tensor", xn_dst, xn_dst, g.gammabc[:], OP.mult,
          sem=dve, ser=True)


def _ffn_p1(g, es_b, b):
    nc, sc = g.nc, g.sc
    pe, act, dve = g.pe, g.act, g.dve
    PS_A, PS_B = g.PS_A, g.PS_B
    xn_tm = g.xn_tm
    seb = [g.S["seb0"], g.S["seb1"]]
    zsp = [g.S["zsp0"], g.S["zsp1"]]
    slsp = [g.S["slsp0"], g.S["slsp1"]]

    with contextlib.ExitStack() as es:
        E = es.enter_context
        PS_X2 = E(nc.psum_tensor(f"PS_X2_{b}", [128, 2048], BF16))
        xn_fm = E(nc.sbuf_tensor(f"xn_fm_{b}", [128, NDC * L], BF16))
        cacc = E(nc.sbuf_tensor(f"cacc_{b}", [128, 2 * L], F32))
        zs = E(nc.sbuf_tensor(f"zs_{b}", [128, 2 * L], F32))
        du = E(nc.sbuf_tensor(f"du_{b}", [128, L], BF16))
        dbf = E(nc.sbuf_tensor(f"dbf_{b}", [128, 2 * L], BF16))
        dtm = E(nc.sbuf_tensor(f"dtm_{b}", [128, 2 * 16 * 512], BF16))
        sebuf = E(nc.sbuf_tensor(f"sebuf_{b}", [128, 2 * 1024], BF16))
        sl_st = E(nc.sbuf_tensor(f"sl_st_{b}", [128, 2 * 4096], BF16))
        xf = lambda dc, s0, n: xn_fm[:, dc * L + s0: dc * L + s0 + n]

        sc.wait("sync", pe)
        sc.wait("sync", act)
        sc.wait("sync", dve)
        t_psx = [0, 0]
        for scn in range(NSEQC):
            ph = scn % 2
            psx = PS_X2[:, ph * 1024:(ph + 1) * 1024]
            for dc in range(NDC):
                sc.op("tensor", "transpose", psx[:, 128 * dc:128 * (dc + 1)],
                      xn_tm[:, scn * D + 128 * dc:scn * D + 128 * (dc + 1)],
                      g.identb[:],
                      waits=((act, t_psx[ph]),) if dc == 0 else (), sem=pe)
            sc.op("scalar", "copy",
                  xn_fm.rearrange("p (dc s) -> p dc s", dc=NDC)[
                      :, :, 128 * scn:128 * (scn + 1)],
                  psx.rearrange("p (dc c) -> p dc c", dc=NDC),
                  waits=(pe,), sem=act)
            t_psx[ph] = act.n
        t_xnfm = act.n

        t_se = [0, 0]
        t_zps = [0, 0]
        t_zs_dve = [0, 0]
        t_zs_act = [0, 0]
        t_du_rd = 0
        t_dbf_rd = [0, 0]
        t_dtm_rd = [0, 0]
        t_dtm_wr = {}
        t_psx_rd = [0, 0]
        t_slots_ev = [0, 0]
        t_cacc = [0, 0]

        def emit_tr(j):
            hf = (j // 4) % 2
            jj = j % 4
            dbfs = dbf[:, (j % 2) * L:(j % 2 + 1) * L]
            for wv2 in range(2):
                ph = wv2 % 2
                psx = PS_X2[:, ph * 1024:(ph + 1) * 1024]
                for scw in range(8):
                    scn = wv2 * 8 + scw
                    sc.op("tensor", "transpose",
                          psx[:, 128 * scw:128 * (scw + 1)],
                          dbfs[:, 128 * scn:128 * (scn + 1)], g.identb[:],
                          waits=((dve, t_dbf_ok[j % 2]), (act, t_psx_rd[ph]),
                                 (pe, t_dtm_rd[hf]))
                          if scw == 0 else (), sem=pe)
                sc.op("scalar", "copy",
                      dtm.rearrange("p (h s jj c) -> p h s jj c",
                                    h=2, s=16, jj=4)[
                          :, hf, wv2 * 8:(wv2 + 1) * 8, jj],
                      psx.rearrange("p (s c) -> p s c", s=8),
                      waits=(pe,), sem=act)
                t_psx_rd[ph] = act.n
            t_dbf_rd[j % 2] = pe.n
            t_dtm_wr[j] = act.n

        def emit_slots(q, i):
            hf = q % 2
            half = i % 2
            for dc in (2 * i, 2 * i + 1):
                for scn in range(NSEQC):
                    w = []
                    if dc == 2 * i and scn == 0:
                        w = [(act, t_dtm_wr[4 * q + 3]),
                             (act, t_slots_ev[half])]
                    sc.op("tensor", "matmul",
                          PS_A[:, (dc % 4) * 512:(dc % 4) * 512 + 512],
                          xn_tm[:, scn * D + dc * 128:scn * D + (dc + 1) * 128],
                          dtm[:, (hf * 16 + scn) * 512:(hf * 16 + scn) * 512 + 512],
                          start=(scn == 0), stop=(scn == NSEQC - 1),
                          waits=w, sem=pe)
            if i == 3:
                t_dtm_rd[hf] = pe.n
            rs = q % 2
            sc.op("scalar", "copy",
                  sl_st[:, rs * 4096 + 1024 * i:rs * 4096 + 1024 * (i + 1)],
                  PS_A[:, (2 * i % 4) * 512:(2 * i % 4) * 512 + 1024],
                  waits=((pe, pe.n), (slsp[rs], slsp[rs].n)), sem=act)
            t_slots_ev[half] = act.n
            if i == 3:
                sc.op("sync", "dma_start", g.sl_hbm[q],
                      sl_st[:, rs * 4096:(rs + 1) * 4096],
                      waits=(act,), sem=slsp[rs], k=16)

        t_dbf_ok = [0, 0]

        def seb_load(j2):
            r2 = j2 % 2
            sc.op("sync", "dma_start", sebuf[:, r2 * 1024:(r2 + 1) * 1024],
                  g.se_hbm[j2],
                  waits=((pe, t_se[r2]),), sem=seb[r2], k=16)

        seb_load(0)
        for j in range(NSC):
            r = j % 2
            zsl = zs[:, r * L:(r + 1) * L]
            if j + 1 < NSC:
                seb_load(j + 1)
            for st in range(4):
                zr = st % 2
                ps = PS_B[:, 512 * zr:512 * (zr + 1)]
                for dc in range(NDC):
                    w = [(seb[r], seb[r].n)]
                    if dc == 0:
                        w += [(act, t_zps[zr]), (act, t_xnfm)]
                    sc.op("tensor", "matmul", ps,
                          sebuf[:, r * 1024 + dc * 128:r * 1024 + (dc + 1) * 128],
                          xf(dc, 512 * st, 512),
                          start=(dc == 0), stop=(dc == NDC - 1),
                          waits=w if dc == 0 else (), sem=pe)
                if st % 2 == 1:
                    sc.op("scalar", "copy", zsl[:, 512 * (st - 1):512 * (st + 1)],
                          PS_B[:, 0:1024],
                          waits=((pe, pe.n), (dve, t_zs_dve[r]),
                                 (act, t_zs_act[r]), (zsp[r], zsp[r].n)),
                          sem=act)
                    t_zps[0] = act.n
                    t_zps[1] = act.n
            t_se[r] = pe.n
            t_zrdy = act.n
            if j > 0:
                emit_tr(j - 1)
            if j >= 4:
                emit_slots(j // 4 - 1, j % 4)
            sc.op("sync", "dma_start", g.z_hbm[j], zsl,
                  waits=((act, t_zrdy),), sem=zsp[r], k=16)
            if j == 0:
                sc.op("vector", "tensor_copy", cacc[:, 0:L], zsl,
                      waits=((act, t_zrdy),), sem=dve)
            else:
                sc.op("vector", "tensor_tensor", cacc[:, r * L:(r + 1) * L],
                      zsl, cacc[:, (1 - r) * L:(2 - r) * L], OP.max,
                      waits=((act, t_zrdy), (dve, t_cacc[r])), sem=dve, ser=True)
            t_cacc[1 - r] = dve.n
            sc.op("vector", "tensor_reduce", g.rows[:, 32:33], zsl, AX.X,
                  OP.max, negate=True, waits=((act, t_zrdy),), sem=dve)
            t_zs_dve[r] = dve.n
            sc.op("scalar", "activation", du[:], zsl, AF.Exp,
                  bias=g.rows[:, 32:33], accum_out=g.rows[:, 33:34],
                  waits=((dve, dve.n), (dve, t_du_rd)), sem=act)
            t_zs_act[r] = act.n
            sc.op("vector", "reciprocal", g.rows[:, 34:35], g.rows[:, 33:34],
                  waits=(act,), sem=dve)
            sc.op("vector", "tensor_scalar", dbf[:, r * L:(r + 1) * L], du[:],
                  g.rows[:, 34:35], None, OP.mult,
                  waits=((pe, t_dbf_rd[r]),), sem=dve, ser=True)
            t_dbf_ok[r] = dve.n
            t_du_rd = dve.n
        emit_tr(NSC - 1)
        for i in range(4):
            emit_slots(NSC // 4 - 1, i)

        fin = cacc[:, ((NSC - 1) % 2) * L:(((NSC - 1) % 2) + 1) * L]
        for st16 in range(NSEQC):
            sc.op("tensor", "transpose", PS_B[:, 0:128],
                  fin[:, 128 * st16:128 * (st16 + 1)], g.identf[:],
                  waits=((act, act.n), (dve, dve.n)), sem=pe)
            sc.op("vector", "tensor_reduce", g.rows[:, 36:37], PS_B[:, 0:128],
                  AX.X, OP.max, waits=(pe,), sem=dve)
            sc.op("tensor", "transpose", PS_B[0:1, 128:256], g.rows[:, 36:37],
                  g.identf[:], waits=(dve,), sem=pe)
            sc.op("vector", "tensor_copy", g.trow[0:1, 0:128],
                  PS_B[0:1, 128:256], waits=(pe,), sem=dve)
            sc.op("tensor", "matmul", PS_A[:, 128 * st16:128 * (st16 + 1)],
                  g.ones_row_f[0:1, :], g.trow[0:1, 0:128],
                  start=(st16 % 4 == 0), stop=(st16 % 4 == 3),
                  waits=((dve, dve.n), (act, max(t_slots_ev))), sem=pe)
        sc.op("scalar", "copy", g.b_m[:], PS_A[:], waits=(pe,), sem=act)
        for s2 in (seb[0], seb[1], zsp[0], zsp[1], slsp[0], slsp[1]):
            sc.wait("sync", s2)
        sc.emit(f"b{b}_p1")


def _ffn_p2(g, es_p2, b):
    nc, sc = g.nc, g.sc
    pe, act, dve = g.pe, g.act, g.dve
    PS_A, PS_B = g.PS_A, g.PS_B
    slb_s = [g.S["slb0"], g.S["slb1"]]
    psp = [g.S["psp0"], g.S["psp1"]]
    E = es_p2.enter_context
    w1b = E(nc.sbuf_tensor(f"w1b_{b}", [128, NDC * 4096], BF16))
    w2b = E(nc.sbuf_tensor(f"w2b_{b}", [128, HIDC * 1024], BF16))
    slb = E(nc.sbuf_tensor(f"slb_{b}", [128, 2 * 4096], BF16))
    hb = E(nc.sbuf_tensor(f"hb_{b}", [128, 2 * 512], BF16))
    p_st = E(nc.sbuf_tensor(f"p_st_{b}", [128, 2 * 2048], BF16))

    for eng in ("gpsimd", "sync"):
        sc.wait(eng, pe)
        sc.wait(eng, act)
        sc.wait(eng, dve)
    w1s = [g.S["w1a"], g.S["w1b"], g.S["w1c"], g.S["w1d"]]
    w2s = [g.S["w2a"], g.S["w2b"], g.S["w2c"], g.S["w2d"]]
    for quarter in range(4):
        for dc in range(2 * quarter, 2 * quarter + 2):
            sc.op("gpsimd", "dma_start", w1b[:, dc * 4096:(dc + 1) * 4096],
                  g.w1[128 * dc:128 * (dc + 1), :], sem=w1s[quarter], k=16)
        for hc in range(8 * quarter, 8 * quarter + 8):
            sc.op("gpsimd", "dma_start", w2b[:, hc * 1024:(hc + 1) * 1024],
                  g.w2[128 * hc:128 * (hc + 1), :], sem=w2s[quarter], k=16)
    t_w1q = [s4.n for s4 in w1s]
    t_w2q = [s4.n for s4 in w2s]

    t_sl_rd = [0, 0]
    t_h_rd = [0, 0]
    t_hps = [0, 0]
    t_pev = 0
    sc.op("sync", "dma_start", slb[:, 0:4096], g.sl_hbm[0],
          waits=((pe, t_sl_rd[0]),), sem=slb_s[0], k=16)
    for jp in range(16):
        q = jp // 2
        r = q % 2
        sl = slb[:, r * 4096:(r + 1) * 4096]
        if jp % 2 == 0 and q + 1 < 8:
            rn = (q + 1) % 2
            sc.op("sync", "dma_start", slb[:, rn * 4096:(rn + 1) * 4096],
                  g.sl_hbm[q + 1],
                  waits=((pe, t_sl_rd[rn]),), sem=slb_s[rn], k=16)
        t_gelu = {}

        def p2_pmm(hc):
            base = ((hc // 2) % 2) * 512 + (hc % 2) * 256
            for jj in range(2):
                for dh in range(2):
                    w = []
                    if jj == 0 and dh == 0:
                        w = [(act, t_gelu[hc]), (w2s[hc // 8], t_w2q[hc // 8])]
                        if hc == 0:
                            w.append((act, t_pev))
                    sc.op("tensor", "matmul",
                          PS_A[:, jj * 1024 + dh * 512:jj * 1024 + (dh + 1) * 512],
                          hb[:, base + jj * 128:base + (jj + 1) * 128],
                          w2b[:, hc * 1024 + dh * 512:hc * 1024 + (dh + 1) * 512],
                          start=(hc == 0), stop=(hc == HIDC - 1),
                          waits=w, sem=pe)
            t_h_rd[(hc // 2) % 2] = pe.n

        for hc in range(HIDC):
            bk = (hc // 2) % 2
            ps_h = PS_B[:, 512 * bk + 256 * (hc % 2):512 * bk + 256 * (hc % 2) + 256]
            for dc in range(NDC):
                w = [(slb_s[r], slb_s[r].n), (w1s[dc // 2], t_w1q[dc // 2])]
                if dc == 0:
                    w += [(act, t_hps[bk])]
                sc.op("tensor", "matmul", ps_h,
                      w1b[:, dc * 4096 + 128 * hc:dc * 4096 + 128 * (hc + 1)],
                      sl[:, dc * 512 + (jp % 2) * 256:dc * 512 + (jp % 2) * 256 + 256],
                      start=(dc == 0), stop=(dc == NDC - 1),
                      waits=w, sem=pe)
            t_sl_rd[r] = pe.n
            if hc % 2 == 1:
                sc.op("scalar", "activation",
                      hb[:, bk * 512:(bk + 1) * 512],
                      PS_B[:, 512 * bk:512 * bk + 512],
                      AF.Gelu, waits=((pe, pe.n), (pe, t_h_rd[bk])), sem=act)
                t_hps[bk] = act.n
                t_gelu[hc - 1] = act.n
                t_gelu[hc] = act.n
            if hc > 1:
                p2_pmm(hc - 2)
        p2_pmm(HIDC - 2)
        p2_pmm(HIDC - 1)
        sc.op("scalar", "copy", p_st[:, r * 2048:(r + 1) * 2048], PS_A[:],
              waits=((pe, pe.n), (psp[r], psp[r].n)), sem=act)
        t_pev = act.n
        sc.op("sync", "dma_start", g.p_hbm[jp], p_st[:, r * 2048:(r + 1) * 2048],
              waits=(act,), sem=psp[r], k=16)
    for s2 in (slb_s[0], slb_s[1], psp[0], psp[1]):
        sc.wait("sync", s2)
    for s2 in w1s + w2s:
        sc.wait("gpsimd", s2)
    sc.emit(f"b{b}_p2")


def _ffn_p3(g, es_p3, b):
    nc, sc = g.nc, g.sc
    pe, act, dve = g.pe, g.act, g.dve
    PS_A, PS_B = g.PS_A, g.PS_B
    szs = [g.S["zs0"], g.S["zs1"], g.S["zs2"], g.S["zs3"]]
    sx2l = [g.S["x2l0"], g.S["x2l1"]]
    syst = [g.S["yst0"], g.S["yst1"]]
    E = es_p3.enter_context
    PS_C = E(nc.psum_tensor(f"PS_Cp_{b}", [128, 512], F32))
    PS_D = E(nc.psum_tensor(f"PS_Dp_{b}", [128, 256], F32))
    p_res = E(nc.sbuf_tensor(f"p_res_{b}", [128, 16 * 2048], BF16))
    zsl = E(nc.sbuf_tensor(f"zsl_{b}", [128, 4 * 256], F32))
    csub = E(nc.sbuf_tensor(f"csub_{b}", [128, 2 * 256], F32))
    cu = E(nc.sbuf_tensor(f"cu_{b}", [128, 2 * 256], BF16))
    bsr = E(nc.sbuf_tensor(f"bsr_{b}", [128, 256], F32))
    ffnraw = E(nc.sbuf_tensor(f"ffnraw_{b}", [128, 2 * 2048], F32))
    ffn_fm = E(nc.sbuf_tensor(f"ffn_fm_{b}", [128, 2048], F32))
    x2sl = E(nc.sbuf_tensor(f"x2sl_{b}", [128, 2 * D], F32))
    x1b = E(nc.sbuf_tensor(f"x1b_p3_{b}", [128, D], F32))
    yb = E(nc.sbuf_tensor(f"yb_{b}", [128, 2 * D], F32))

    sc.wait("sync", pe)
    sc.wait("sync", act)
    sc.wait("sync", dve)
    spl4 = [g.S["pld"], g.S["pld1"], g.S["pld2"], g.S["pld3"]]
    for jp in range(16):
        sc.op("sync", "dma_start", p_res[:, jp * 2048:(jp + 1) * 2048],
              g.p_hbm[jp], sem=spl4[jp // 4], k=16)
    t_p4 = [s4.n for s4 in spl4]
    pr = lambda j, dc: p_res[:, (j // 2) * 2048 + (j % 2) * 1024 + dc * 128:
                             (j // 2) * 2048 + (j % 2) * 1024 + (dc + 1) * 128]

    t_z_rd = [0] * 4
    t_cs_rd = [0, 0]
    t_cu_rd = [0, 0]
    t_psa_free = 0
    t_srow_dve = 0
    t_srow_ok = [0, 0]
    t_srow_act = [0, 0]
    t_raw_rd = [0, 0]
    t_mult2 = [0]
    t_bsr_rd = 0
    t_ps_t = 0
    t_x2 = [0, 0]
    t_y = [0, 0]
    t_ffn_rd = 0
    zdma = {}

    def emit_fin_a(stp):
        nonlocal t_bsr_rd, t_ffn_rd, t_ps_t
        rr = stp % 2
        srow = g.trow[0:1, 256 * rr:256 * rr + 256]
        jrow = g.junk[0:1, 256 * rr:256 * rr + 256]
        sc.op("scalar", "activation", srow, srow, AF.Ln,
              waits=((dve, t_srow_ok[rr]),), sem=act)
        sc.op("scalar", "activation", jrow, srow, AF.Exp, scale=-1.0,
              sem=act, ser=True)
        t_srow_act[rr] = act.n
        sc.op("tensor", "matmul", PS_D[:, 0:256], g.ones_row_b[0:1, :], jrow,
              start=True, stop=True,
              waits=((act, act.n), (dve, t_bsr_rd)), sem=pe)
        sc.op("vector", "tensor_copy", bsr[:, 0:256], PS_D[:, 0:256],
              waits=(pe,), sem=dve, ser=True)
        raw = ffnraw[:, rr * 2048:(rr + 1) * 2048]
        for dc in range(NDC):
            sc.op("vector", "tensor_tensor", ffn_fm[:, dc * 256:(dc + 1) * 256],
                  raw[:, dc * 256:(dc + 1) * 256], bsr[:, 0:256], OP.mult,
                  waits=((pe, t_ffn_rd),) if dc == 0 else (), sem=dve, ser=True)
        t_mult2[0] = dve.n
        t_raw_rd[rr] = dve.n
        t_bsr_rd = dve.n
        for sc2 in range(2):
            scn = stp * 2 + sc2
            r = scn % 2
            sc.op("sync", "dma_start", x2sl[:, r * D:(r + 1) * D],
                  g.x2_hbm[scn], waits=((dve, t_x2[r]),), sem=sx2l[r], k=16)

    def emit_fin_b(stp):
        nonlocal t_ffn_rd, t_ps_t
        for sc2 in range(2):
            scn = stp * 2 + sc2
            r = scn % 2
            for dc in range(NDC):
                w = []
                if dc == 0:
                    w = [(dve, t_mult2[0]), (dve, t_ps_t)]
                sc.op("tensor", "transpose", PS_B[:, 128 * dc:128 * (dc + 1)],
                      ffn_fm[:, dc * 256 + sc2 * 128:dc * 256 + sc2 * 128 + 128],
                      g.identf[:], waits=w, sem=pe)
            t_ffn_rd = pe.n
            sc.op("vector", "tensor_tensor", x1b[:], x2sl[:, r * D:(r + 1) * D],
                  PS_B[:], OP.add, waits=((pe, pe.n), (sx2l[r], sx2l[r].n)),
                  sem=dve, ser=True)
            t_ps_t = dve.n
            t_x2[r] = dve.n
            _ln_only(g, x1b, yb[:, r * D:(r + 1) * D],
                     waits_y=((syst[r], syst[r].n),))
            sc.op("sync", "dma_start", g.y_out[b, 128 * scn:128 * (scn + 1), :],
                  yb[:, r * D:(r + 1) * D], waits=(dve,), sem=syst[r], k=16)
            t_y[r] = syst[r].n
        return

    def _zload(st, j):
        r4 = j % 4
        sc.op("sync", "dma_start", zsl[:, r4 * 256:(r4 + 1) * 256],
              g.z_hbm[j][:, 256 * st:256 * (st + 1)],
              waits=((dve, t_z_rd[r4]),), sem=szs[r4], k=16)
        zdma[(st, j)] = (szs[r4], szs[r4].n)

    _zload(0, 0)
    _zload(0, 1)
    _zload(0, 2)
    for st in range(8):
        for j in range(NSC):
            r4 = j % 4
            nst, nj = st, j + 3
            if nj >= NSC:
                nst, nj = st + 1, nj - NSC
            if nst < 8:
                _zload(nst, nj)
            r2 = j % 2
            sc.op("vector", "tensor_tensor", csub[:, r2 * 256:(r2 + 1) * 256],
                  zsl[:, r4 * 256:(r4 + 1) * 256],
                  g.b_m[:, 256 * st:256 * (st + 1)], OP.subtract,
                  waits=(zdma[(st, j)], (act, t_cs_rd[r2])), sem=dve)
            t_z_rd[r4] = dve.n
            sc.op("scalar", "activation", cu[:, r2 * 256:(r2 + 1) * 256],
                  csub[:, r2 * 256:(r2 + 1) * 256], AF.Exp,
                  waits=((dve, dve.n), (pe, t_cu_rd[r2])), sem=act)
            t_cs_rd[r2] = act.n
            for dc in range(NDC):
                w = []
                if dc == 0:
                    w = [(act, act.n), (spl4[j // 8], t_p4[j // 8])]
                    if j == 0:
                        w += [(act, t_psa_free)]
                sc.op("tensor", "matmul", PS_A[:, dc * 256:(dc + 1) * 256],
                      pr(j, dc), cu[:, r2 * 256:(r2 + 1) * 256],
                      start=(j == 0 and dc % 2 == 0),
                      stop=(j == NSC - 1 and dc % 2 == 1),
                      waits=w, sem=pe)
            sc.op("tensor", "matmul", PS_C[0:1, 0:256], g.ones_col_b[:],
                  cu[:, r2 * 256:(r2 + 1) * 256],
                  start=(j == 0), stop=(j == NSC - 1),
                  waits=((dve, t_srow_dve),) if j == 0 else (),
                  sem=pe)
            t_cu_rd[r2] = pe.n
            if j == 2 and st > 0:
                emit_fin_a(st - 1)
            if j == 4 and st > 0:
                emit_fin_b(st - 1)
        rr = st % 2
        sc.op("vector", "tensor_copy", g.trow[0:1, 256 * rr:256 * rr + 256],
              PS_C[0:1, 0:256],
              waits=((pe, pe.n), (act, t_srow_act[rr])), sem=dve)
        t_srow_dve = dve.n
        t_srow_ok[rr] = dve.n
        sc.op("scalar", "copy", ffnraw[:, rr * 2048:(rr + 1) * 2048], PS_A[:],
              waits=((pe, pe.n), (dve, t_raw_rd[rr])), sem=act)
        t_psa_free = act.n
    emit_fin_a(7)
    emit_fin_b(7)
    for s2 in (spl4[0], spl4[1], spl4[2], spl4[3], szs[0], szs[1], szs[2],
               szs[3], sx2l[0], sx2l[1], syst[0], syst[1]):
        sc.wait("sync", s2)
    sc.emit(f"b{b}_p3")


def _ln_only(g, x1, y_dst, waits_y=()):
    sc, dve, act = g.sc, g.dve, g.act
    sc.op("vector", "bn_stats", g.rows[:, 48:54], x1[:, 0:512], sem=dve, ser=True)
    sc.op("vector", "bn_stats", g.rows[:, 54:60], x1[:, 512:1024], sem=dve,
          ser=True)
    sc.op("vector", "bn_aggr", g.rows[:, 60:62], g.rows[:, 48:60], sem=dve,
          ser=True)
    mu, var = g.rows[:, 60:61], g.rows[:, 61:62]
    R = lambda i: g.rows[:, 62 + i:63 + i]
    sc.op("scalar", "activation", R(0), var, AF.Ln, bias=g.epsc[:, 1:2],
          waits=(dve,), sem=act)
    sc.op("scalar", "activation", R(1), R(0), AF.Exp, scale=-0.5, sem=act,
          ser=True)
    sc.op("vector", "tensor_scalar", y_dst, x1[:], mu, R(1),
          OP.subtract, OP.mult, waits=((act, act.n),) + tuple(waits_y), sem=dve,
          ser=True)



def _prep_inputs(inputs):
    x = np.ascontiguousarray(inputs["L0_attn_inputs"])
    w_qkv = np.ascontiguousarray(inputs["L0_attn_W_qkv"])
    w_out = np.ascontiguousarray(inputs["L0_attn_W_out"])
    w1 = np.ascontiguousarray(inputs["L0_ffn_w1"])
    w2 = np.ascontiguousarray(inputs["L0_ffn_w2"])
    sl = np.ascontiguousarray(inputs["L0_ffn_slot_embeds"].reshape(NS, D))
    gx = np.ascontiguousarray(inputs["L0_ffn_gamma_x"].reshape(1, D))
    gs = np.ascontiguousarray(inputs["L0_ffn_gamma_slot"].reshape(8, 128).T)
    identf = np.eye(128, dtype=np.float32)
    identb = np.eye(128, dtype=ml_dtypes.bfloat16)
    trib = np.triu(np.ones((128, 128), dtype=np.float32)).astype(ml_dtypes.bfloat16)
    shared = dict(w_qkv=w_qkv, w_out=w_out, w1=w1, w2=w2, sl_emb=sl,
                  gx_row=gx, gs_t=gs, c_identf=identf, c_identb=identb,
                  c_trib=trib)
    return x, shared


def kernel(**inputs):
    x, shared = _prep_inputs(inputs)
    nc = build_program(nb=NB, debug=False)
    in_maps = []
    for c in range(8):
        m = dict(shared)
        m["x_in"] = np.ascontiguousarray(x[NB * c:NB * (c + 1)])
        in_maps.append(m)
    res = run_bass_kernel_spmd(nc, in_maps, core_ids=list(range(8)))
    return np.concatenate([r["y_out"] for r in res.results], axis=0)


# version 22
# speedup vs baseline: 1.0691x; 1.0691x over previous
import contextlib
import numpy as np
import ml_dtypes
import concourse.bass as bass
import concourse.mybir as mybir
from concourse.alu_op_type import AluOpType as OP
from concourse.bass_utils import run_bass_kernel_spmd

F32 = mybir.dt.float32
BF16 = mybir.dt.bfloat16
AF = mybir.ActivationFunctionType
AX = mybir.AxisListType

NB = 4
L = 2048
D = 1024
NDC = 8
NSEQC = 16
NQT = 4
H = 16
NHP = 8
NS = 4096
NSC = 32
HIDC = 32
BND_SCALE = -0.126


class Sem:
    def __init__(self, h):
        self.h = h
        self.n = 0

    def bump(self, k=1):
        self.n += k
        return self.n


class Script:
    ENGS = ("sync", "tensor", "vector", "scalar", "gpsimd")

    def __init__(self, nc):
        self.nc = nc
        self.q = {e: [] for e in self.ENGS}
        self._waited = {e: {} for e in self.ENGS}

    def _wait(self, eng, sem, val):
        if val is None or val <= 0:
            return
        w = self._waited[eng]
        if w.get(sem, -1) >= val:
            return
        w[sem] = val

        def f(e, h=sem.h, v=val):
            e.wait_ge(h, v)
        self.q[eng].append(f)

    def op(self, eng, method, *args, waits=(), sem=None, k=1, ser=False, **kw):
        if ser:
            own = self.eng2sem.get(eng)
            if own is not None:
                self._wait(eng, own, own.n)
        for wspec in waits:
            if isinstance(wspec, tuple):
                s, v = wspec
            else:
                s, v = wspec, wspec.n
            self._wait(eng, s, v)
        out_n = None
        if sem is not None:
            out_n = sem.bump(k)

        def f(e, method=method, args=args, kw=kw, sem=sem, k=k):
            r = getattr(e, method)(*args, **kw)
            if sem is not None:
                r.then_inc(sem.h, k)
        self.q[eng].append(f)
        return out_n

    def wait(self, eng, sem, val=None):
        self._wait(eng, sem, sem.n if val is None else val)

    def emit(self, name=None):
        with self.nc.Block(name) as blk:
            for eng in self.ENGS:
                fns = self.q[eng]
                if not fns:
                    continue
                getattr(blk, eng)(lambda e, fns=fns: [f(e) for f in fns])
        self.q = {e: [] for e in self.ENGS}
        self._waited = {e: {} for e in self.ENGS}


class Ctx:
    pass


def build_program(nb=NB, debug=False):
    nc = bass.Bass("TRN2", target_bir_lowering=False)
    g = Ctx()
    g.nc = nc
    g.nb = nb

    g.x_in = nc.dram_tensor("x_in", [nb, L, D], F32, kind="ExternalInput").ap()
    g.w_qkv = nc.dram_tensor("w_qkv", [D, 3 * D], F32, kind="ExternalInput").ap()
    g.w_out = nc.dram_tensor("w_out", [D, D], F32, kind="ExternalInput").ap()
    g.w1 = nc.dram_tensor("w1", [D, 4 * D], F32, kind="ExternalInput").ap()
    g.w2 = nc.dram_tensor("w2", [4 * D, D], F32, kind="ExternalInput").ap()
    g.sl_emb = nc.dram_tensor("sl_emb", [NS, D], F32, kind="ExternalInput").ap()
    g.gx_row = nc.dram_tensor("gx_row", [1, D], F32, kind="ExternalInput").ap()
    g.gs_t = nc.dram_tensor("gs_t", [128, 8], F32, kind="ExternalInput").ap()
    g.c_identf = nc.dram_tensor("c_identf", [128, 128], F32, kind="ExternalInput").ap()
    g.c_identb = nc.dram_tensor("c_identb", [128, 128], BF16, kind="ExternalInput").ap()
    g.c_trib = nc.dram_tensor("c_trib", [128, 128], BF16, kind="ExternalInput").ap()
    g.y_out = nc.dram_tensor("y_out", [nb, L, D], F32, kind="ExternalOutput").ap()

    dbg = "ExternalOutput" if debug else "Internal"
    g.se_hbm = nc.dram_tensor("se_hbm", [NSC, 128, 1024], BF16, kind=dbg).ap()
    g.z_hbm = nc.dram_tensor("z_hbm", [NSC, 128, L], F32, kind=dbg).ap()
    g.x2_hbm = nc.dram_tensor("x2_hbm", [NSEQC, 128, D], F32, kind=dbg).ap()
    g.sl_hbm = nc.dram_tensor("sl_hbm", [8, 128, 4096], BF16, kind=dbg).ap()
    g.p_hbm = nc.dram_tensor("p_hbm", [16, 128, 2048], BF16, kind=dbg).ap()

    with contextlib.ExitStack() as es:
        E = es.enter_context

        g.pe = Sem(E(nc.semaphore("pe")))
        g.act = Sem(E(nc.semaphore("act")))
        g.dve = Sem(E(nc.semaphore("dve")))
        g.gpe = Sem(E(nc.semaphore("gpe")))
        names_sync = ["cst", "seld0", "seld1", "sest0", "sest1", "x0", "x1",
                      "x2sp", "seb0", "seb1", "zsp0", "zsp1", "slsp0", "slsp1",
                      "slb0", "slb1", "psp0", "psp1", "pld", "pld1", "pld2", "pld3", "zs0", "zs1",
                      "zs2", "zs3", "x2l0", "x2l1", "yst0", "yst1"]
        names_gp = ["wq", "wk", "wv", "wo", "w1g", "w2g",
                    "w1a", "w1b", "w1c", "w1d", "w2a", "w2b", "w2c", "w2d"]
        g.S = {n2: Sem(E(nc.semaphore(n2))) for n2 in names_sync + names_gp}

        g.identf = E(nc.sbuf_tensor("identf", [128, 128], F32))
        g.identb = E(nc.sbuf_tensor("identb", [128, 128], BF16))
        g.trib = E(nc.sbuf_tensor("trib", [128, 128], BF16))
        g.ones_row_f = E(nc.sbuf_tensor("ones_row_f", [64, 128], F32))
        g.ones_col_b = E(nc.sbuf_tensor("ones_col_b", [128, 1], BF16))
        g.ones_row_b = E(nc.sbuf_tensor("ones_row_b", [64, 128], BF16))
        g.gammabc = E(nc.sbuf_tensor("gammabc", [128, D], F32))
        g.gst = E(nc.sbuf_tensor("gst", [128, 8], F32))
        g.junk = E(nc.sbuf_tensor("junk", [128, 1024], BF16))
        g.epsc = E(nc.sbuf_tensor("epsc", [128, 2], F32))
        g.rows = E(nc.sbuf_tensor("rows", [128, 64], F32))
        g.trow = E(nc.sbuf_tensor("trow", [64, 1024], F32))
        g.bias_col = E(nc.sbuf_tensor("bias_col", [128, 1], F32))

        g.PS_A = E(nc.psum_tensor("PS_A", [128, 2048], F32))
        g.PS_B = E(nc.psum_tensor("PS_B", [128, 1024], F32))

        sc = Script(nc)
        sc.eng2sem = {"tensor": g.pe, "vector": g.dve, "scalar": g.act,
                      "gpsimd": g.gpe}
        g.sc = sc

        _s0(g, es)

        for b in range(nb):
            with contextlib.ExitStack() as es_b:
                _batch_attn_p1(g, es_b, b)
                with contextlib.ExitStack() as es_p2:
                    _ffn_p2(g, es_p2, b)
                with contextlib.ExitStack() as es_p3:
                    _ffn_p3(g, es_p3, b)

        sc.wait("sync", g.S["yst0"])
        sc.wait("sync", g.S["yst1"])
        sc.emit("fin")

    return nc


def _s0(g, es):
    nc, sc = g.nc, g.sc
    pe, act, dve, gpe = g.pe, g.act, g.dve, g.gpe
    cst = g.S["cst"]

    sc.op("sync", "dma_start", g.identf[:], g.c_identf, sem=cst, k=16)
    sc.op("sync", "dma_start", g.identb[:], g.c_identb, sem=cst, k=16)
    sc.op("sync", "dma_start", g.trib[:], g.c_trib, sem=cst, k=16)
    sc.op("sync", "dma_start", g.gst[:], g.gs_t, sem=cst, k=16)
    sc.op("gpsimd", "memset", g.ones_row_f[:], 1.0, sem=gpe)
    sc.op("gpsimd", "memset", g.ones_col_b[:], 1.0, sem=gpe)
    sc.op("gpsimd", "memset", g.ones_row_b[:], 1.0, sem=gpe)
    sc.op("gpsimd", "memset", g.epsc[:, 0:1], 1e-30, sem=gpe)
    sc.op("gpsimd", "memset", g.epsc[:, 1:2], 1e-5, sem=gpe)
    with contextlib.ExitStack() as es00:
        gxr = es00.enter_context(nc.sbuf_tensor("gxr", [1, D], F32))
        sc.op("sync", "dma_start", gxr[:], g.gx_row, sem=cst, k=16)
        t_gx = cst.n
        for i in range(2):
            sc.op("tensor", "matmul", g.PS_B[:, 512 * i:512 * (i + 1)],
                  g.ones_row_f[0:1, :], gxr[0:1, 512 * i:512 * (i + 1)],
                  start=True, stop=True,
                  waits=((cst, t_gx), (gpe, 4)), sem=pe)
        sc.op("scalar", "copy", g.gammabc[:], g.PS_B[:, 0:1024], waits=(pe,), sem=act)
        sc.emit("s0_const")

    with contextlib.ExitStack() as es0:
        E0 = es0.enter_context
        PS_X = E0(nc.psum_tensor("PS_X0", [128, 1024], BF16))
        srows = E0(nc.sbuf_tensor("srows", [128, 16], F32))
        se_tm = E0(nc.sbuf_tensor("se_tm", [128, 2 * D], F32))
        se_bf = E0(nc.sbuf_tensor("se_bf", [128, 2 * D], BF16))
        se_st = E0(nc.sbuf_tensor("se_st", [128, 2 * D], BF16))
        seld = [g.S["seld0"], g.S["seld1"]]
        sest = [g.S["sest0"], g.S["sest1"]]
        t_rd = [0, 0]
        t_tr = [0, 0]
        t_bf_ok = [0, 0]
        t_rd_act = [0, 0]
        t_ps = [0]

        def s0_tail(c):
            r = c % 2
            bf = se_bf[:, r * D:(r + 1) * D]
            st = se_st[:, r * D:(r + 1) * D]
            for dc in range(8):
                sc.op("tensor", "transpose", PS_X[:, 128 * dc:128 * (dc + 1)],
                      bf[:, 128 * dc:128 * (dc + 1)], g.identb[:],
                      waits=((dve, t_bf_ok[r]), (act, t_ps[0])), sem=pe)
            t_tr[r] = pe.n
            for dc in range(8):
                sc.op("scalar", "activation", st[:, 128 * dc:128 * (dc + 1)],
                      PS_X[:, 128 * dc:128 * (dc + 1)], AF.Copy,
                      scale=g.gst[:, dc:dc + 1],
                      waits=((pe, t_tr[r]), (cst, t_gx), (sest[r], sest[r].n)),
                      sem=act)
            t_ps[0] = act.n
            sc.op("sync", "dma_start", g.se_hbm[c], st,
                  waits=(act,), sem=sest[r], k=16)

        for c in range(NSC):
            r = c % 2
            tm = se_tm[:, r * D:(r + 1) * D]
            bf = se_bf[:, r * D:(r + 1) * D]
            sc.op("sync", "dma_start", tm, g.sl_emb[128 * c:128 * (c + 1), :],
                  waits=((dve, t_rd[r]), (act, t_rd_act[r])), sem=seld[r], k=16)
            if c > 0:
                s0_tail(c - 1)
            RS = lambda i: srows[:, 8 * r + i:8 * r + i + 1]
            sc.op("scalar", "activation", g.junk[:, 0:D], tm, AF.Square,
                  accum_out=RS(0), waits=(seld[r],), sem=act)
            t_rd_act[r] = act.n
            sc.op("scalar", "activation", RS(1), RS(0), AF.Ln,
                  bias=g.epsc[:, 0:1], sem=act, ser=True)
            sc.op("scalar", "activation", RS(2), RS(1), AF.Exp,
                  scale=0.5, sem=act, ser=True)
            sc.op("vector", "tensor_scalar", RS(3), RS(2),
                  1e-12, None, OP.max, waits=(act,), sem=dve)
            sc.op("vector", "reciprocal", RS(4), RS(3), sem=dve,
                  ser=True)
            sc.op("vector", "tensor_scalar", RS(5), RS(4),
                  32.0, None, OP.mult, sem=dve, ser=True)
            sc.op("vector", "tensor_scalar", bf, tm, RS(5), None,
                  OP.mult, waits=((pe, t_tr[r]),), sem=dve, ser=True)
            t_bf_ok[r] = dve.n
            t_rd[r] = dve.n
        s0_tail(NSC - 1)
        for s2 in (seld[0], seld[1], sest[0], sest[1]):
            sc.wait("sync", s2)
        sc.emit("s0_se")


def _batch_attn_p1(g, es_b, b):
    nc, sc = g.nc, g.sc
    pe, act, dve, gpe = g.pe, g.act, g.dve, g.gpe
    PS_A, PS_B = g.PS_A, g.PS_B
    sx = [g.S["x0"], g.S["x1"]]
    es_qkv = contextlib.ExitStack()
    EB = es_qkv.enter_context
    PS_C = EB(nc.psum_tensor(f"PS_Ca_{b}", [128, 512], F32))

    q_fm = EB(nc.sbuf_tensor(f"q_fm_{b}", [128, NHP * L], BF16))
    k_fm = EB(nc.sbuf_tensor(f"k_fm_{b}", [128, NHP * L], BF16))
    v_tm = EB(nc.sbuf_tensor(f"v_tm_{b}", [128, NSEQC * 1040], BF16))
    x_tm = EB(nc.sbuf_tensor(f"x_tm_{b}", [128, 2 * D], F32))

    qf = lambda hp, s0, n: q_fm[:, hp * L + s0: hp * L + s0 + n]
    kf = lambda hp, s0, n: k_fm[:, hp * L + s0: hp * L + s0 + n]
    vt = lambda scn, c0, n: v_tm[:, scn * 1040 + c0: scn * 1040 + c0 + n]

    for eng in ("gpsimd", "sync"):
        sc.wait(eng, pe)
        sc.wait(eng, act)
        sc.wait(eng, dve)
    g.b_m = es_b.enter_context(
        nc.sbuf_tensor(f"b_m_{b}", [128, L], F32, side="right"))
    es_xn = contextlib.ExitStack()
    for scn in range(NSEQC):
        sc.op("gpsimd", "memset",
              v_tm.rearrange("p (s h c) -> p s h c", s=NSEQC, h=16)[:, scn, :, 64:65],
              1.0, sem=gpe)
    t_vones = gpe.n

    with contextlib.ExitStack() as es_a1:
        EA = es_a1.enter_context
        PS_Cb = EA(nc.psum_tensor(f"PS_Cb_{b}", [128, 512], F32))
        x_bf = EA(nc.sbuf_tensor(f"x_bf_{b}", [128, NDC * L], BF16))
        wbuf = EA(nc.sbuf_tensor(f"wbuf_{b}", [128, 2 * NDC * 1024], BF16))
        ksq = EA(nc.sbuf_tensor(f"ksq_{b}", [128, 2 * L], BF16))
        wv = lambda r, dc: wbuf[:, (r * NDC + dc) * 1024:(r * NDC + dc + 1) * 1024]
        xb = lambda dc, s0, n: x_bf[:, dc * L + s0: dc * L + s0 + n]

        t_free = [0, 0]
        t_psA = [0, 0]
        t_xbf_st = {}

        def x_load2(p2):
            sc.op("sync", "dma_start",
                  x_tm.rearrange("p (two d) -> p two d", two=2),
                  g.x_in[b, 256 * p2:256 * (p2 + 1), :].rearrange(
                      "(two p) d -> p two d", two=2),
                  waits=((pe, t_free[0]),), sem=sx[p2 % 2], k=16)

        x_load2(0)
        for scn in range(NSEQC):
            r = scn % 2
            pair = scn // 2
            xt = x_tm[:, r * D:(r + 1) * D]
            psx = PS_A[:, r * 1024:(r + 1) * 1024]
            for dc in range(NDC):
                sc.op("tensor", "transpose", psx[:, 128 * dc:128 * (dc + 1)],
                      xt[:, 128 * dc:128 * (dc + 1)], g.identf[:],
                      waits=((sx[pair % 2], sx[pair % 2].n), (dve, t_psA[r])),
                      sem=pe)
            if r == 1:
                t_free[0] = pe.n
                if pair + 1 < NSEQC // 2:
                    x_load2(pair + 1)
            sc.op("vector", "tensor_copy",
                  x_bf.rearrange("p (dc s) -> p dc s", dc=NDC)[
                      :, :, 128 * scn:128 * (scn + 1)],
                  psx.rearrange("p (dc c) -> p dc c", dc=NDC),
                  waits=(pe,), sem=dve)
            t_psA[r] = dve.n
            if scn % 4 == 3:
                t_xbf_st[scn // 4] = dve.n

        swq, swk, swv = g.S["wq"], g.S["wk"], g.S["wv"]
        for dc in range(NDC):
            sc.op("gpsimd", "dma_start", wv(0, dc),
                  g.w_qkv[128 * dc:128 * (dc + 1), 0:1024], sem=swq, k=16)
        for dc in range(NDC):
            sc.op("gpsimd", "dma_start", wv(1, dc),
                  g.w_qkv[128 * dc:128 * (dc + 1), 1024:2048], sem=swk, k=16)

        t_ev = [0, 0]
        for (name, ws, wr, dst) in (("q", swq, 0, qf), ("k", swk, 1, kf)):
            for st in range(4):
                for hp in range(NHP):
                    r = (hp + st * NHP) % 2
                    ps = PS_B[:, 512 * r:512 * (r + 1)]
                    for dc in range(NDC):
                        w = ((ws, ws.n), (act, t_ev[r]),
                             (dve, t_xbf_st[st])) if dc == 0 else ()
                        sc.op("tensor", "matmul", ps,
                              wv(wr, dc)[:, 128 * hp:128 * (hp + 1)],
                              xb(dc, 512 * st, 512),
                              start=(dc == 0), stop=(dc == NDC - 1),
                              waits=w, sem=pe)
                    sc.op("scalar", "copy", dst(hp, 512 * st, 512), ps,
                          waits=(pe,), sem=act)
                    t_ev[r] = act.n
            if name == "q":
                sc.wait("gpsimd", pe)
                for dc in range(NDC):
                    sc.op("gpsimd", "dma_start", wv(0, dc),
                          g.w_qkv[128 * dc:128 * (dc + 1), 2048:3072],
                          sem=swv, k=16)

        t_evv = [0, 0]
        for scn in range(NSEQC):
            for vf in range(2):
                r = (scn * 2 + vf) % 2
                ps = PS_B[:, 512 * r:512 * (r + 1)]
                for dc in range(NDC):
                    w = [(swv, swv.n)]
                    if dc == 0:
                        w += [(dve, t_evv[r]), (act, t_ev[r]), (gpe, t_vones)]
                    sc.op("tensor", "matmul", ps,
                          xb(dc, 128 * scn, 128),
                          wv(0, dc)[:, 512 * vf:512 * (vf + 1)],
                          start=(dc == 0), stop=(dc == NDC - 1),
                          waits=w if dc == 0 else (), sem=pe)
                sc.op("vector", "tensor_copy",
                      v_tm.rearrange("p (s h c) -> p s h c", s=NSEQC, h=16)[
                          :, scn, vf * 8:(vf + 1) * 8, 0:64],
                      ps.rearrange("p (h c) -> p h c", h=8),
                      waits=(pe,), sem=dve)
                t_evv[r] = dve.n

        t_ksq2 = [0, 0]
        t_psn_rd = [0, 0]
        idx = 0
        for ti, src in enumerate((qf, kf)):
            first = True
            rr = 0
            for hp in range(NHP):
                kk = ksq[:, (hp % 2) * L:(hp % 2 + 1) * L]
                sc.op("vector", "tensor_tensor", kk, src(hp, 0, L),
                      src(hp, 0, L), OP.mult,
                      waits=((pe, t_ksq2[hp % 2]), (act, t_ev[0]),
                             (act, t_ev[1])),
                      sem=dve)
                t_ksq_ok = dve.n
                for st in range(4):
                    idx2 = (hp * 4 + st) % 2
                    psn = PS_C if idx2 == 0 else PS_Cb
                    sc.op("tensor", "matmul", psn[0:1, 0:512],
                          g.ones_col_b[:], kk[:, 512 * st:512 * (st + 1)],
                          start=True, stop=True,
                          waits=((dve, t_ksq_ok), (dve, t_psn_rd[idx2])),
                          sem=pe)
                    t_ksq2[hp % 2] = pe.n
                    nrr = 1 - rr
                    if first:
                        sc.op("vector", "tensor_copy",
                              g.trow[32 * nrr:32 * nrr + 1, 0:512],
                              psn[0:1, 0:512], waits=(pe,), sem=dve, ser=True)
                        first = False
                    else:
                        sc.op("vector", "tensor_tensor",
                              g.trow[32 * nrr:32 * nrr + 1, 0:512],
                              psn[0:1, 0:512], g.trow[32 * rr:32 * rr + 1, 0:512],
                              OP.max, waits=(pe,), sem=dve, ser=True)
                    t_psn_rd[idx2] = dve.n
                    rr = nrr
            sc.op("vector", "tensor_reduce", g.rows[0:1, 8 + ti:9 + ti],
                  g.trow[32 * rr:32 * rr + 1, 0:512], AX.X, OP.max, sem=dve,
                  ser=True)
        sc.op("vector", "tensor_tensor", g.rows[0:1, 10:11], g.rows[0:1, 8:9],
              g.rows[0:1, 9:10], OP.mult, sem=dve, ser=True)
        sc.op("scalar", "activation", g.rows[0:1, 11:12], g.rows[0:1, 10:11],
              AF.Ln, bias=g.epsc[0:1, 0:1], waits=(dve,), sem=act)
        sc.op("scalar", "activation", g.rows[0:1, 12:13], g.rows[0:1, 11:12],
              AF.Exp, scale=0.5, sem=act, ser=True)
        sc.op("vector", "tensor_scalar", g.rows[0:1, 13:14], g.rows[0:1, 12:13],
              BND_SCALE, None, OP.mult, waits=(act,), sem=dve)
        sc.op("tensor", "matmul", PS_C[:, 0:1], g.ones_row_f[0:1, :],
              g.rows[0:1, 13:14], start=True, stop=True, waits=(dve,), sem=pe)
        sc.op("vector", "tensor_copy", g.bias_col[:], PS_C[:, 0:1],
              waits=(pe,), sem=dve)
        t_bias = dve.n
        sc.emit(f"b{b}_a15")

    with contextlib.ExitStack() as es_a2:
        EA = es_a2.enter_context
        PS_X = EA(nc.psum_tensor(f"PS_Xa_{b}", [128, 1024], BF16))
        wout = EA(nc.sbuf_tensor(f"wout_{b}", [128, NDC * 1024], BF16))
        attnout = EA(nc.sbuf_tensor(f"attnout_{b}", [128, NHP * 512], BF16))
        ebuf = EA(nc.sbuf_tensor(f"ebuf_{b}", [128, 2 * 2048], BF16))
        proj_fm = EA(nc.sbuf_tensor(f"proj_fm_{b}", [128, NDC * 512], BF16))
        x1b = EA(nc.sbuf_tensor(f"x1b_{b}", [128, D], F32))
        x2b = EA(nc.sbuf_tensor(f"x2b_{b}", [128, D], F32))
        bcnt = EA(nc.sbuf_tensor(f"bcnt_{b}", [128, 512], BF16))
        bcn = bcnt[:, :]
        avb = EA(nc.sbuf_tensor(f"avb_{b}", [128, 2 * 512], BF16))
        xn_tm = es_xn.enter_context(
            nc.sbuf_tensor(f"xn_tm_{b}", [128, NSEQC * D], BF16, side="right"))
        g.xn_tm = xn_tm

        swo = g.S["wo"]
        for fc in range(NDC):
            sc.op("gpsimd", "dma_start", wout[:, fc * 1024:(fc + 1) * 1024],
                  g.w_out[128 * fc:128 * (fc + 1), :], sem=swo, k=16)
        t_wo = swo.n

        ps_s2 = [PS_A[:, 0:1024], PS_A[:, 1024:2048]]
        ps_av2 = [PS_B[:, 0:512], PS_B[:, 512:1024]]
        t_exp = [0, 0]
        t_av_rd = [0, 0]
        t_avev = [0, 0]
        t_e_rd = {}
        t_bcn_rd = [0, 0]
        t_bcn_dve = [0, 0]
        t_projA = 0
        t_projB = 0
        t_psx_rd = 0
        t_xres = [0, 0]
        t_at_rd = 0

        def eslot(ha, gi):
            i = (gi % 2) * 2 + ha
            return ebuf[:, i * 1024:(i + 1) * 1024]

        def emit_scores(qt, hp, ha, gi):
            rbase = 64 * ha
            for ci in range(2):
                kc = 2 * gi + ci
                off = max(0, 128 * kc - 512 * qt)
                sc.op("tensor", "matmul",
                      ps_s2[ha][:, 512 * ci + off:512 * (ci + 1)],
                      kf(hp, 128 * kc, 128)[rbase:rbase + 64, :],
                      qf(hp, 512 * qt + off, 512 - off)[rbase:rbase + 64, :],
                      start=True, stop=True,
                      waits=((act, t_exp[ha]),) if ci == 0 else (), sem=pe)

        def emit_exp(qt, hp, ha, gi, t_s):
            eb = eslot(ha, gi)
            diag = (128 * (2 * gi) >= 512 * qt)
            ew = ((pe, t_s), (pe, t_e_rd.get((ha, gi % 2), 0)), (dve, t_bias))
            if not diag:
                sc.op("scalar", "activation", eb[:], ps_s2[ha][:],
                      AF.Exp, bias=g.bias_col[:, 0:1], scale=0.125,
                      waits=ew, sem=act)
            else:
                for ci in range(2):
                    kc = 2 * gi + ci
                    off = max(0, 128 * kc - 512 * qt)
                    sc.op("scalar", "activation",
                          eb[:, 512 * ci + off:512 * (ci + 1)],
                          ps_s2[ha][:, 512 * ci + off:512 * (ci + 1)],
                          AF.Exp, bias=g.bias_col[:, 0:1], scale=0.125,
                          waits=ew, sem=act)
                for ci in range(2):
                    kc = 2 * gi + ci
                    off = max(0, 128 * kc - 512 * qt)
                    sc.op("vector", "tensor_tensor",
                          eb[:, 512 * ci + off:512 * ci + off + 128],
                          eb[:, 512 * ci + off:512 * ci + off + 128],
                          g.trib[:], OP.mult, waits=(act,), sem=dve)
            t_exp[ha] = act.n

        def emit_av(qt, hp, ha, gi, G):
            hg = 2 * hp + ha
            eb = eslot(ha, gi)
            diag = (128 * (2 * gi) >= 512 * qt)
            for ci in range(2):
                kc = 2 * gi + ci
                off = max(0, 128 * kc - 512 * qt)
                w = [(act, t_exp[ha])]
                if diag:
                    w.append((dve, dve.n))
                if gi == 0 and ci == 0:
                    w += [(act, t_avev[ha]),
                          (act, t_projA if ha == 0 else t_projB)]
                sc.op("tensor", "matmul",
                      ps_av2[ha][0:65, off:512],
                      vt(kc, 65 * hg, 65),
                      eb[:, 512 * ci + off:512 * (ci + 1)],
                      start=(gi == 0 and ci == 0),
                      stop=(gi == G - 1 and ci == 1),
                      waits=w, sem=pe)
            t_e_rd[(ha, gi % 2)] = pe.n

        pending = [None]

        def flush_finale():
            if pending[0] is None:
                return
            hp0, t_ln = pending[0]
            pending[0] = None
            slot = (hp0 % 2) * 512
            sl2 = avb[:, slot:slot + 512]
            for ha in range(2):
                sc.op("tensor", "matmul",
                      PS_C[64 * ha:64 * ha + 64, 0:512],
                      g.ones_row_b[32 * ha:32 * ha + 1, 0:64],
                      g.junk[32 * ha:32 * ha + 1,
                             512 * (hp0 % 2):512 * (hp0 % 2) + 512],
                      start=True, stop=True,
                      waits=((act, t_ln), (dve, t_bcn_dve[0]))
                      if ha == 0 else (), sem=pe)
            t_bcn_rd[0] = pe.n
            sc.op("vector", "tensor_copy", bcn, PS_C[0:128, 0:512],
                  waits=(pe,), sem=dve, ser=True)
            for ha in range(2):
                rbase = 64 * ha
                sc.op("vector", "tensor_tensor",
                      attnout[rbase:rbase + 64, hp0 * 512:(hp0 + 1) * 512],
                      sl2[rbase:rbase + 64, :], bcn[rbase:rbase + 64, :],
                      OP.mult, waits=((pe, t_at_rd),), sem=dve, ser=True)
                t_av_rd[ha] = dve.n
            t_bcn_dve[0] = dve.n

        for qt in range(NQT):
            G = 2 * (qt + 1)
            for hp in range(NHP):
                for gi in range(G):
                    emit_scores(qt, hp, 0, gi)
                    emit_scores(qt, hp, 1, gi)
                    t_s = pe.n
                    if gi == 0:
                        flush_finale()
                    if gi > 0:
                        emit_av(qt, hp, 0, gi - 1, G)
                        emit_av(qt, hp, 1, gi - 1, G)
                    diag = (128 * (2 * gi) >= 512 * qt)
                    if not diag:
                        par = gi % 2
                        eb2 = ebuf[:, par * 2048:(par + 1) * 2048]
                        ew = ((pe, t_s), (pe, t_e_rd.get((0, par), 0)),
                              (pe, t_e_rd.get((1, par), 0)), (dve, t_bias))
                        sc.op("scalar", "activation", eb2[:], PS_A[:],
                              AF.Exp, bias=g.bias_col[:, 0:1], scale=0.125,
                              waits=ew, sem=act)
                        t_exp[0] = act.n
                        t_exp[1] = act.n
                    else:
                        emit_exp(qt, hp, 0, gi, t_s)
                        emit_exp(qt, hp, 1, gi, t_s)
                emit_av(qt, hp, 0, G - 1, G)
                emit_av(qt, hp, 1, G - 1, G)
                slot = (hp % 2) * 512
                sl2 = avb[:, slot:slot + 512]
                for ha in range(2):
                    sc.op("vector", "tensor_copy", sl2[64 * ha:64 * ha + 64, :],
                          ps_av2[ha][0:64, 0:512],
                          waits=((pe, pe.n), (dve, t_av_rd[ha])), sem=dve)
                    sc.op("vector", "tensor_copy",
                          g.trow[32 * ha:32 * ha + 1,
                                 512 * (hp % 2):512 * (hp % 2) + 512],
                          ps_av2[ha][64:65, 0:512],
                          waits=((pe, t_bcn_rd[0]),), sem=dve)
                    t_avev[ha] = dve.n
                for ha in range(2):
                    rrow = g.trow[32 * ha:32 * ha + 1,
                                  512 * (hp % 2):512 * (hp % 2) + 512]
                    sc.op("scalar", "activation", rrow, rrow, AF.Ln,
                          waits=((dve, t_avev[1]), (pe, t_bcn_rd[0])), sem=act)
                    sc.op("scalar", "activation",
                          g.junk[32 * ha:32 * ha + 1,
                                 512 * (hp % 2):512 * (hp % 2) + 512],
                          rrow, AF.Exp, scale=-1.0, sem=act, ser=True)
                pending[0] = (hp, act.n)
            flush_finale()
            t_heads = dve.n

            for dc in range(NDC):
                pr = dc % 2
                ps_p = PS_B[:, 512 * pr:512 * (pr + 1)]
                w = [(swo, t_wo), (dve, t_heads),
                     (act, t_projA if pr == 0 else t_projB)]
                for fc in range(NDC):
                    sc.op("tensor", "matmul", ps_p,
                          wout[:, fc * 1024 + 128 * dc:fc * 1024 + 128 * (dc + 1)],
                          attnout[:, fc * 512:(fc + 1) * 512],
                          start=(fc == 0), stop=(fc == NDC - 1),
                          waits=w if fc == 0 else (), sem=pe)
                t_at_rd = pe.n
                sc.op("scalar", "copy", proj_fm[:, dc * 512:(dc + 1) * 512], ps_p,
                      waits=(pe,), sem=act)
                if pr == 0:
                    t_projA = act.n
                else:
                    t_projB = act.n

            def xres_load(scl2):
                scn2 = 4 * qt + scl2
                r2 = scn2 % 2
                sc.op("sync", "dma_start", x_tm[:, r2 * D:(r2 + 1) * D],
                      g.x_in[b, 128 * scn2:128 * (scn2 + 1), :],
                      waits=((dve, t_xres[r2]),), sem=sx[r2], k=16)

            xres_load(0)
            xres_load(1)
            t_x1_free = getattr(g, "t_x1_free", 0)
            for scl in range(4):
                scn = 4 * qt + scl
                r = scn % 2
                xr = x_tm[:, r * D:(r + 1) * D]
                for dc in range(NDC):
                    sc.op("tensor", "transpose", PS_X[:, 128 * dc:128 * (dc + 1)],
                          proj_fm[:, dc * 512 + 128 * scl:dc * 512 + 128 * (scl + 1)],
                          g.identb[:],
                          waits=((act, max(t_projA, t_projB)), (dve, t_psx_rd))
                          if dc == 0 else (), sem=pe)
                sc.op("vector", "tensor_tensor", x1b[:], xr, PS_X[:],
                      OP.add, waits=((pe, pe.n), (sx[r], sx[r].n),
                                     (dve, t_x1_free)), sem=dve)
                t_psx_rd = dve.n
                t_xres[r] = dve.n
                if scl < 2:
                    xres_load(scl + 2)
                sx2 = g.S["x2sp"]
                _ln_rms(g, x1b, x2b[:],
                        xn_dst=xn_tm[:, scn * D:(scn + 1) * D],
                        waits_x2=((sx2, sx2.n),))
                sc.op("sync", "dma_start", g.x2_hbm[scn],
                      x2b[:], waits=(dve,), sem=sx2, k=16)
                t_x1_free = g.t_x1_free
        for s2 in (sx[0], sx[1], g.S["x2sp"]):
            sc.wait("sync", s2)
        for s2 in (swq, swk, swv, swo):
            sc.wait("gpsimd", s2)
        sc.emit(f"b{b}_a2")
    es_qkv.close()

    _ffn_p1(g, es_b, b)
    es_xn.close()


def _ln_rms(g, x1, x2_dst, xn_dst, waits_x2=()):
    sc, dve, act = g.sc, g.dve, g.act
    R = lambda i: g.rows[:, 16 + i:17 + i]
    sc.op("vector", "bn_stats", g.rows[:, 16:22], x1[:, 0:512], sem=dve, ser=True)
    sc.op("vector", "bn_stats", g.rows[:, 22:28], x1[:, 512:1024], sem=dve,
          ser=True)
    sc.op("vector", "bn_aggr", g.rows[:, 28:30], g.rows[:, 16:28], sem=dve,
          ser=True)
    mu, var = g.rows[:, 28:29], g.rows[:, 29:30]
    R = lambda i: g.rows[:, 30 + i:31 + i]
    sc.op("scalar", "activation", R(6), var, AF.Ln, bias=g.epsc[:, 1:2],
          waits=(dve,), sem=act)
    sc.op("scalar", "activation", R(7), R(6), AF.Exp, scale=-0.5, sem=act,
          ser=True)
    sc.op("scalar", "activation", R(8), var, AF.Ln, bias=g.epsc[:, 0:1], sem=act)
    sc.op("scalar", "activation", R(9), R(8), AF.Exp, scale=0.5, sem=act, ser=True)
    sc.op("vector", "tensor_scalar", x2_dst, x1[:], mu, R(7),
          OP.subtract, OP.mult, waits=((act, act.n),) + tuple(waits_x2), sem=dve,
          ser=True)
    g.t_x1_free = sc.eng2sem["vector"].n
    sc.op("vector", "tensor_tensor", R(10), R(9), R(7), OP.mult, sem=dve, ser=True)
    sc.op("vector", "tensor_scalar", R(11), R(10), 32.0, None, OP.mult, sem=dve,
          ser=True)
    sc.op("vector", "tensor_scalar", R(12), R(11), 1e-12, None, OP.max, sem=dve,
          ser=True)
    sc.op("vector", "reciprocal", R(13), R(12), sem=dve, ser=True)
    sc.op("vector", "tensor_scalar", R(14), R(13), 32.0, None, OP.mult, sem=dve,
          ser=True)
    sc.op("vector", "tensor_scalar", xn_dst, x2_dst, R(14), None, OP.mult,
          sem=dve, ser=True)
    sc.op("vector", "tensor_tensor", xn_dst, xn_dst, g.gammabc[:], OP.mult,
          sem=dve, ser=True)


def _ffn_p1(g, es_b, b):
    nc, sc = g.nc, g.sc
    pe, act, dve = g.pe, g.act, g.dve
    PS_A, PS_B = g.PS_A, g.PS_B
    xn_tm = g.xn_tm
    seb = [g.S["seb0"], g.S["seb1"]]
    zsp = [g.S["zsp0"], g.S["zsp1"]]
    slsp = [g.S["slsp0"], g.S["slsp1"]]

    with contextlib.ExitStack() as es:
        E = es.enter_context
        PS_X2 = E(nc.psum_tensor(f"PS_X2_{b}", [128, 2048], BF16))
        xn_fm = E(nc.sbuf_tensor(f"xn_fm_{b}", [128, NDC * L], BF16))
        cacc = E(nc.sbuf_tensor(f"cacc_{b}", [128, 2 * L], F32))
        zs = E(nc.sbuf_tensor(f"zs_{b}", [128, 2 * L], F32))
        du = E(nc.sbuf_tensor(f"du_{b}", [128, L], BF16))
        dbf = E(nc.sbuf_tensor(f"dbf_{b}", [128, 2 * L], BF16))
        dtm = E(nc.sbuf_tensor(f"dtm_{b}", [128, 2 * 16 * 512], BF16))
        sebuf = E(nc.sbuf_tensor(f"sebuf_{b}", [128, 2 * 1024], BF16))
        sl_st = E(nc.sbuf_tensor(f"sl_st_{b}", [128, 2 * 4096], BF16))
        xf = lambda dc, s0, n: xn_fm[:, dc * L + s0: dc * L + s0 + n]

        sc.wait("sync", pe)
        sc.wait("sync", act)
        sc.wait("sync", dve)
        t_psx = [0, 0]
        for scn in range(NSEQC):
            ph = scn % 2
            psx = PS_X2[:, ph * 1024:(ph + 1) * 1024]
            for dc in range(NDC):
                sc.op("tensor", "transpose", psx[:, 128 * dc:128 * (dc + 1)],
                      xn_tm[:, scn * D + 128 * dc:scn * D + 128 * (dc + 1)],
                      g.identb[:],
                      waits=((act, t_psx[ph]),) if dc == 0 else (), sem=pe)
            sc.op("scalar", "copy",
                  xn_fm.rearrange("p (dc s) -> p dc s", dc=NDC)[
                      :, :, 128 * scn:128 * (scn + 1)],
                  psx.rearrange("p (dc c) -> p dc c", dc=NDC),
                  waits=(pe,), sem=act)
            t_psx[ph] = act.n
        t_xnfm = act.n

        t_se = [0, 0]
        t_zps = [0, 0]
        t_zs_dve = [0, 0]
        t_zs_act = [0, 0]
        t_du_rd = 0
        t_dbf_rd = [0, 0]
        t_dtm_rd = [0, 0]
        t_dtm_wr = {}
        t_psx_rd = [0, 0]
        t_slots_ev = [0, 0]
        t_cacc = [0, 0]

        def emit_tr(j):
            hf = (j // 4) % 2
            jj = j % 4
            dbfs = dbf[:, (j % 2) * L:(j % 2 + 1) * L]
            for wv2 in range(2):
                ph = wv2 % 2
                psx = PS_X2[:, ph * 1024:(ph + 1) * 1024]
                for scw in range(8):
                    scn = wv2 * 8 + scw
                    sc.op("tensor", "transpose",
                          psx[:, 128 * scw:128 * (scw + 1)],
                          dbfs[:, 128 * scn:128 * (scn + 1)], g.identb[:],
                          waits=((dve, t_dbf_ok[j % 2]), (act, t_psx_rd[ph]),
                                 (pe, t_dtm_rd[hf]))
                          if scw == 0 else (), sem=pe)
                sc.op("scalar", "copy",
                      dtm.rearrange("p (h s jj c) -> p h s jj c",
                                    h=2, s=16, jj=4)[
                          :, hf, wv2 * 8:(wv2 + 1) * 8, jj],
                      psx.rearrange("p (s c) -> p s c", s=8),
                      waits=(pe,), sem=act)
                t_psx_rd[ph] = act.n
            t_dbf_rd[j % 2] = pe.n
            t_dtm_wr[j] = act.n

        def emit_slots(q, i):
            hf = q % 2
            half = i % 2
            for dc in (2 * i, 2 * i + 1):
                for scn in range(NSEQC):
                    w = []
                    if dc == 2 * i and scn == 0:
                        w = [(act, t_dtm_wr[4 * q + 3]),
                             (act, t_slots_ev[half])]
                    sc.op("tensor", "matmul",
                          PS_A[:, (dc % 4) * 512:(dc % 4) * 512 + 512],
                          xn_tm[:, scn * D + dc * 128:scn * D + (dc + 1) * 128],
                          dtm[:, (hf * 16 + scn) * 512:(hf * 16 + scn) * 512 + 512],
                          start=(scn == 0), stop=(scn == NSEQC - 1),
                          waits=w, sem=pe)
            if i == 3:
                t_dtm_rd[hf] = pe.n
            rs = q % 2
            sc.op("scalar", "copy",
                  sl_st[:, rs * 4096 + 1024 * i:rs * 4096 + 1024 * (i + 1)],
                  PS_A[:, (2 * i % 4) * 512:(2 * i % 4) * 512 + 1024],
                  waits=((pe, pe.n), (slsp[rs], slsp[rs].n)), sem=act)
            t_slots_ev[half] = act.n
            if i == 3:
                sc.op("sync", "dma_start", g.sl_hbm[q],
                      sl_st[:, rs * 4096:(rs + 1) * 4096],
                      waits=(act,), sem=slsp[rs], k=16)

        t_dbf_ok = [0, 0]

        def seb_load(j2):
            r2 = j2 % 2
            sc.op("sync", "dma_start", sebuf[:, r2 * 1024:(r2 + 1) * 1024],
                  g.se_hbm[j2],
                  waits=((pe, t_se[r2]),), sem=seb[r2], k=16)

        seb_load(0)
        for j in range(NSC):
            r = j % 2
            zsl = zs[:, r * L:(r + 1) * L]
            if j + 1 < NSC:
                seb_load(j + 1)
            for st in range(4):
                zr = st % 2
                ps = PS_B[:, 512 * zr:512 * (zr + 1)]
                for dc in range(NDC):
                    w = [(seb[r], seb[r].n)]
                    if dc == 0:
                        w += [(act, t_zps[zr]), (act, t_xnfm)]
                    sc.op("tensor", "matmul", ps,
                          sebuf[:, r * 1024 + dc * 128:r * 1024 + (dc + 1) * 128],
                          xf(dc, 512 * st, 512),
                          start=(dc == 0), stop=(dc == NDC - 1),
                          waits=w if dc == 0 else (), sem=pe)
                if st % 2 == 1:
                    sc.op("scalar", "copy", zsl[:, 512 * (st - 1):512 * (st + 1)],
                          PS_B[:, 0:1024],
                          waits=((pe, pe.n), (dve, t_zs_dve[r]),
                                 (act, t_zs_act[r]), (zsp[r], zsp[r].n)),
                          sem=act)
                    t_zps[0] = act.n
                    t_zps[1] = act.n
            t_se[r] = pe.n
            t_zrdy = act.n
            if j > 0:
                emit_tr(j - 1)
            if j >= 4:
                emit_slots(j // 4 - 1, j % 4)
            sc.op("sync", "dma_start", g.z_hbm[j], zsl,
                  waits=((act, t_zrdy),), sem=zsp[r], k=16)
            if j == 0:
                sc.op("vector", "tensor_copy", cacc[:, 0:L], zsl,
                      waits=((act, t_zrdy),), sem=dve)
            else:
                sc.op("vector", "tensor_tensor", cacc[:, r * L:(r + 1) * L],
                      zsl, cacc[:, (1 - r) * L:(2 - r) * L], OP.max,
                      waits=((act, t_zrdy), (dve, t_cacc[r])), sem=dve, ser=True)
            t_cacc[1 - r] = dve.n
            sc.op("vector", "tensor_reduce", g.rows[:, 32:33], zsl, AX.X,
                  OP.max, negate=True, waits=((act, t_zrdy),), sem=dve)
            t_zs_dve[r] = dve.n
            sc.op("scalar", "activation", du[:], zsl, AF.Exp,
                  bias=g.rows[:, 32:33], accum_out=g.rows[:, 33:34],
                  waits=((dve, dve.n), (dve, t_du_rd)), sem=act)
            t_zs_act[r] = act.n
            sc.op("vector", "reciprocal", g.rows[:, 34:35], g.rows[:, 33:34],
                  waits=(act,), sem=dve)
            sc.op("vector", "tensor_scalar", dbf[:, r * L:(r + 1) * L], du[:],
                  g.rows[:, 34:35], None, OP.mult,
                  waits=((pe, t_dbf_rd[r]),), sem=dve, ser=True)
            t_dbf_ok[r] = dve.n
            t_du_rd = dve.n
        emit_tr(NSC - 1)
        for i in range(4):
            emit_slots(NSC // 4 - 1, i)

        fin = cacc[:, ((NSC - 1) % 2) * L:(((NSC - 1) % 2) + 1) * L]
        for st16 in range(NSEQC):
            sc.op("tensor", "transpose", PS_B[:, 0:128],
                  fin[:, 128 * st16:128 * (st16 + 1)], g.identf[:],
                  waits=((act, act.n), (dve, dve.n)), sem=pe)
            sc.op("vector", "tensor_reduce", g.rows[:, 36:37], PS_B[:, 0:128],
                  AX.X, OP.max, waits=(pe,), sem=dve)
            sc.op("tensor", "transpose", PS_B[0:1, 128:256], g.rows[:, 36:37],
                  g.identf[:], waits=(dve,), sem=pe)
            sc.op("vector", "tensor_copy", g.trow[0:1, 0:128],
                  PS_B[0:1, 128:256], waits=(pe,), sem=dve)
            sc.op("tensor", "matmul", PS_A[:, 128 * st16:128 * (st16 + 1)],
                  g.ones_row_f[0:1, :], g.trow[0:1, 0:128],
                  start=(st16 % 4 == 0), stop=(st16 % 4 == 3),
                  waits=((dve, dve.n), (act, max(t_slots_ev))), sem=pe)
        sc.op("scalar", "copy", g.b_m[:], PS_A[:], waits=(pe,), sem=act)
        for s2 in (seb[0], seb[1], zsp[0], zsp[1], slsp[0], slsp[1]):
            sc.wait("sync", s2)
        sc.emit(f"b{b}_p1")


def _ffn_p2(g, es_p2, b):
    nc, sc = g.nc, g.sc
    pe, act, dve = g.pe, g.act, g.dve
    PS_A, PS_B = g.PS_A, g.PS_B
    slb_s = [g.S["slb0"], g.S["slb1"]]
    psp = [g.S["psp0"], g.S["psp1"]]
    E = es_p2.enter_context
    w1b = E(nc.sbuf_tensor(f"w1b_{b}", [128, NDC * 4096], BF16))
    w2b = E(nc.sbuf_tensor(f"w2b_{b}", [128, HIDC * 1024], BF16))
    slb = E(nc.sbuf_tensor(f"slb_{b}", [128, 2 * 4096], BF16))
    hb = E(nc.sbuf_tensor(f"hb_{b}", [128, 2 * 512], BF16))
    p_st = E(nc.sbuf_tensor(f"p_st_{b}", [128, 2 * 2048], BF16))

    for eng in ("gpsimd", "sync"):
        sc.wait(eng, pe)
        sc.wait(eng, act)
        sc.wait(eng, dve)
    w1s = [g.S["w1a"], g.S["w1b"], g.S["w1c"], g.S["w1d"]]
    w2s = [g.S["w2a"], g.S["w2b"], g.S["w2c"], g.S["w2d"]]
    for quarter in range(4):
        for dc in range(2 * quarter, 2 * quarter + 2):
            sc.op("gpsimd", "dma_start", w1b[:, dc * 4096:(dc + 1) * 4096],
                  g.w1[128 * dc:128 * (dc + 1), :], sem=w1s[quarter], k=16)
        for hc in range(8 * quarter, 8 * quarter + 8):
            sc.op("gpsimd", "dma_start", w2b[:, hc * 1024:(hc + 1) * 1024],
                  g.w2[128 * hc:128 * (hc + 1), :], sem=w2s[quarter], k=16)
    t_w1q = [s4.n for s4 in w1s]
    t_w2q = [s4.n for s4 in w2s]

    t_sl_rd = [0, 0]
    t_h_rd = [0, 0]
    t_hps = [0, 0]
    t_pev = 0
    sc.op("sync", "dma_start", slb[:, 0:4096], g.sl_hbm[0],
          waits=((pe, t_sl_rd[0]),), sem=slb_s[0], k=16)
    for jp in range(16):
        q = jp // 2
        r = q % 2
        sl = slb[:, r * 4096:(r + 1) * 4096]
        if jp % 2 == 0 and q + 1 < 8:
            rn = (q + 1) % 2
            sc.op("sync", "dma_start", slb[:, rn * 4096:(rn + 1) * 4096],
                  g.sl_hbm[q + 1],
                  waits=((pe, t_sl_rd[rn]),), sem=slb_s[rn], k=16)
        t_gelu = {}

        def p2_pmm(hc):
            base = ((hc // 2) % 2) * 512 + (hc % 2) * 256
            for jj in range(2):
                for dh in range(2):
                    w = []
                    if jj == 0 and dh == 0:
                        w = [(act, t_gelu[hc]), (w2s[hc // 8], t_w2q[hc // 8])]
                        if hc == 0:
                            w.append((act, t_pev))
                    sc.op("tensor", "matmul",
                          PS_A[:, jj * 1024 + dh * 512:jj * 1024 + (dh + 1) * 512],
                          hb[:, base + jj * 128:base + (jj + 1) * 128],
                          w2b[:, hc * 1024 + dh * 512:hc * 1024 + (dh + 1) * 512],
                          start=(hc == 0), stop=(hc == HIDC - 1),
                          waits=w, sem=pe)
            t_h_rd[(hc // 2) % 2] = pe.n

        for hc in range(HIDC):
            bk = (hc // 2) % 2
            ps_h = PS_B[:, 512 * bk + 256 * (hc % 2):512 * bk + 256 * (hc % 2) + 256]
            for dc in range(NDC):
                w = [(slb_s[r], slb_s[r].n), (w1s[dc // 2], t_w1q[dc // 2])]
                if dc == 0:
                    w += [(act, t_hps[bk])]
                sc.op("tensor", "matmul", ps_h,
                      w1b[:, dc * 4096 + 128 * hc:dc * 4096 + 128 * (hc + 1)],
                      sl[:, dc * 512 + (jp % 2) * 256:dc * 512 + (jp % 2) * 256 + 256],
                      start=(dc == 0), stop=(dc == NDC - 1),
                      waits=w, sem=pe)
            t_sl_rd[r] = pe.n
            if hc % 2 == 1:
                sc.op("scalar", "activation",
                      hb[:, bk * 512:(bk + 1) * 512],
                      PS_B[:, 512 * bk:512 * bk + 512],
                      AF.Gelu, waits=((pe, pe.n), (pe, t_h_rd[bk])), sem=act)
                t_hps[bk] = act.n
                t_gelu[hc - 1] = act.n
                t_gelu[hc] = act.n
            if hc > 1:
                p2_pmm(hc - 2)
        p2_pmm(HIDC - 2)
        p2_pmm(HIDC - 1)
        sc.op("scalar", "copy", p_st[:, r * 2048:(r + 1) * 2048], PS_A[:],
              waits=((pe, pe.n), (psp[r], psp[r].n)), sem=act)
        t_pev = act.n
        sc.op("sync", "dma_start", g.p_hbm[jp], p_st[:, r * 2048:(r + 1) * 2048],
              waits=(act,), sem=psp[r], k=16)
    for s2 in (slb_s[0], slb_s[1], psp[0], psp[1]):
        sc.wait("sync", s2)
    for s2 in w1s + w2s:
        sc.wait("gpsimd", s2)
    sc.emit(f"b{b}_p2")


def _ffn_p3(g, es_p3, b):
    nc, sc = g.nc, g.sc
    pe, act, dve = g.pe, g.act, g.dve
    PS_A, PS_B = g.PS_A, g.PS_B
    szs = [g.S["zs0"], g.S["zs1"], g.S["zs2"], g.S["zs3"]]
    sx2l = [g.S["x2l0"], g.S["x2l1"]]
    syst = [g.S["yst0"], g.S["yst1"]]
    E = es_p3.enter_context
    PS_C = E(nc.psum_tensor(f"PS_Cp_{b}", [128, 512], F32))
    PS_D = E(nc.psum_tensor(f"PS_Dp_{b}", [128, 256], F32))
    p_res = E(nc.sbuf_tensor(f"p_res_{b}", [128, 16 * 2048], BF16))
    zsl = E(nc.sbuf_tensor(f"zsl_{b}", [128, 4 * 256], F32))
    csub = E(nc.sbuf_tensor(f"csub_{b}", [128, 2 * 256], F32))
    cu = E(nc.sbuf_tensor(f"cu_{b}", [128, 2 * 256], BF16))
    bsr = E(nc.sbuf_tensor(f"bsr_{b}", [128, 256], F32))
    ffnraw = E(nc.sbuf_tensor(f"ffnraw_{b}", [128, 2 * 2048], F32))
    ffn_fm = E(nc.sbuf_tensor(f"ffn_fm_{b}", [128, 2048], F32))
    x2sl = E(nc.sbuf_tensor(f"x2sl_{b}", [128, 2 * D], F32))
    x1b = E(nc.sbuf_tensor(f"x1b_p3_{b}", [128, D], F32))
    yb = E(nc.sbuf_tensor(f"yb_{b}", [128, 2 * D], F32))

    sc.wait("sync", pe)
    sc.wait("sync", act)
    sc.wait("sync", dve)
    spl4 = [g.S["pld"], g.S["pld1"], g.S["pld2"], g.S["pld3"]]
    for jp in range(16):
        sc.op("sync", "dma_start", p_res[:, jp * 2048:(jp + 1) * 2048],
              g.p_hbm[jp], sem=spl4[jp // 4], k=16)
    t_p4 = [s4.n for s4 in spl4]
    pr = lambda j, dc: p_res[:, (j // 2) * 2048 + (j % 2) * 1024 + dc * 128:
                             (j // 2) * 2048 + (j % 2) * 1024 + (dc + 1) * 128]

    t_z_rd = [0] * 4
    t_cs_rd = [0, 0]
    t_cu_rd = [0, 0]
    t_psa_free = 0
    t_srow_dve = 0
    t_srow_ok = [0, 0]
    t_srow_act = [0, 0]
    t_raw_rd = [0, 0]
    t_mult2 = [0]
    t_bsr_rd = 0
    t_ps_t = 0
    t_x2 = [0, 0]
    t_y = [0, 0]
    t_ffn_rd = 0
    zdma = {}

    def emit_fin_a(stp):
        nonlocal t_bsr_rd, t_ffn_rd, t_ps_t
        rr = stp % 2
        srow = g.trow[0:1, 256 * rr:256 * rr + 256]
        jrow = g.junk[0:1, 256 * rr:256 * rr + 256]
        sc.op("scalar", "activation", srow, srow, AF.Ln,
              waits=((dve, t_srow_ok[rr]),), sem=act)
        sc.op("scalar", "activation", jrow, srow, AF.Exp, scale=-1.0,
              sem=act, ser=True)
        t_srow_act[rr] = act.n
        sc.op("tensor", "matmul", PS_D[:, 0:256], g.ones_row_b[0:1, :], jrow,
              start=True, stop=True,
              waits=((act, act.n), (dve, t_bsr_rd)), sem=pe)
        sc.op("vector", "tensor_copy", bsr[:, 0:256], PS_D[:, 0:256],
              waits=(pe,), sem=dve, ser=True)
        raw = ffnraw[:, rr * 2048:(rr + 1) * 2048]
        for dc in range(NDC):
            sc.op("vector", "tensor_tensor", ffn_fm[:, dc * 256:(dc + 1) * 256],
                  raw[:, dc * 256:(dc + 1) * 256], bsr[:, 0:256], OP.mult,
                  waits=((pe, t_ffn_rd),) if dc == 0 else (), sem=dve, ser=True)
        t_mult2[0] = dve.n
        t_raw_rd[rr] = dve.n
        t_bsr_rd = dve.n
        for sc2 in range(2):
            scn = stp * 2 + sc2
            r = scn % 2
            sc.op("sync", "dma_start", x2sl[:, r * D:(r + 1) * D],
                  g.x2_hbm[scn], waits=((dve, t_x2[r]),), sem=sx2l[r], k=16)

    def emit_fin_b(stp):
        nonlocal t_ffn_rd, t_ps_t
        for sc2 in range(2):
            scn = stp * 2 + sc2
            r = scn % 2
            for dc in range(NDC):
                w = []
                if dc == 0:
                    w = [(dve, t_mult2[0]), (dve, t_ps_t)]
                sc.op("tensor", "transpose", PS_B[:, 128 * dc:128 * (dc + 1)],
                      ffn_fm[:, dc * 256 + sc2 * 128:dc * 256 + sc2 * 128 + 128],
                      g.identf[:], waits=w, sem=pe)
            t_ffn_rd = pe.n
            sc.op("vector", "tensor_tensor", x1b[:], x2sl[:, r * D:(r + 1) * D],
                  PS_B[:], OP.add, waits=((pe, pe.n), (sx2l[r], sx2l[r].n)),
                  sem=dve, ser=True)
            t_ps_t = dve.n
            t_x2[r] = dve.n
            _ln_only(g, x1b, yb[:, r * D:(r + 1) * D],
                     waits_y=((syst[r], syst[r].n),))
            sc.op("sync", "dma_start", g.y_out[b, 128 * scn:128 * (scn + 1), :],
                  yb[:, r * D:(r + 1) * D], waits=(dve,), sem=syst[r], k=16)
            t_y[r] = syst[r].n
        return

    def _zload(st, j):
        r4 = j % 4
        sc.op("sync", "dma_start", zsl[:, r4 * 256:(r4 + 1) * 256],
              g.z_hbm[j][:, 256 * st:256 * (st + 1)],
              waits=((dve, t_z_rd[r4]),), sem=szs[r4], k=16)
        zdma[(st, j)] = (szs[r4], szs[r4].n)

    _zload(0, 0)
    _zload(0, 1)
    _zload(0, 2)
    for st in range(8):
        for j in range(NSC):
            r4 = j % 4
            nst, nj = st, j + 3
            if nj >= NSC:
                nst, nj = st + 1, nj - NSC
            if nst < 8:
                _zload(nst, nj)
            r2 = j % 2
            sc.op("vector", "tensor_tensor", csub[:, r2 * 256:(r2 + 1) * 256],
                  zsl[:, r4 * 256:(r4 + 1) * 256],
                  g.b_m[:, 256 * st:256 * (st + 1)], OP.subtract,
                  waits=(zdma[(st, j)], (act, t_cs_rd[r2])), sem=dve)
            t_z_rd[r4] = dve.n
            sc.op("scalar", "activation", cu[:, r2 * 256:(r2 + 1) * 256],
                  csub[:, r2 * 256:(r2 + 1) * 256], AF.Exp,
                  waits=((dve, dve.n), (pe, t_cu_rd[r2])), sem=act)
            t_cs_rd[r2] = act.n
            for dc in range(NDC):
                w = []
                if dc == 0:
                    w = [(act, act.n), (spl4[j // 8], t_p4[j // 8])]
                    if j == 0:
                        w += [(act, t_psa_free)]
                sc.op("tensor", "matmul", PS_A[:, dc * 256:(dc + 1) * 256],
                      pr(j, dc), cu[:, r2 * 256:(r2 + 1) * 256],
                      start=(j == 0 and dc % 2 == 0),
                      stop=(j == NSC - 1 and dc % 2 == 1),
                      waits=w, sem=pe)
            sc.op("tensor", "matmul", PS_C[0:1, 0:256], g.ones_col_b[:],
                  cu[:, r2 * 256:(r2 + 1) * 256],
                  start=(j == 0), stop=(j == NSC - 1),
                  waits=((dve, t_srow_dve),) if j == 0 else (),
                  sem=pe)
            t_cu_rd[r2] = pe.n
            if j == 2 and st > 0:
                emit_fin_a(st - 1)
            if j == 4 and st > 0:
                emit_fin_b(st - 1)
        rr = st % 2
        sc.op("vector", "tensor_copy", g.trow[0:1, 256 * rr:256 * rr + 256],
              PS_C[0:1, 0:256],
              waits=((pe, pe.n), (act, t_srow_act[rr])), sem=dve)
        t_srow_dve = dve.n
        t_srow_ok[rr] = dve.n
        sc.op("scalar", "copy", ffnraw[:, rr * 2048:(rr + 1) * 2048], PS_A[:],
              waits=((pe, pe.n), (dve, t_raw_rd[rr])), sem=act)
        t_psa_free = act.n
    emit_fin_a(7)
    emit_fin_b(7)
    for s2 in (spl4[0], spl4[1], spl4[2], spl4[3], szs[0], szs[1], szs[2],
               szs[3], sx2l[0], sx2l[1], syst[0], syst[1]):
        sc.wait("sync", s2)
    sc.emit(f"b{b}_p3")


def _ln_only(g, x1, y_dst, waits_y=()):
    sc, dve, act = g.sc, g.dve, g.act
    sc.op("vector", "bn_stats", g.rows[:, 48:54], x1[:, 0:512], sem=dve, ser=True)
    sc.op("vector", "bn_stats", g.rows[:, 54:60], x1[:, 512:1024], sem=dve,
          ser=True)
    sc.op("vector", "bn_aggr", g.rows[:, 60:62], g.rows[:, 48:60], sem=dve,
          ser=True)
    mu, var = g.rows[:, 60:61], g.rows[:, 61:62]
    R = lambda i: g.rows[:, 62 + i:63 + i]
    sc.op("scalar", "activation", R(0), var, AF.Ln, bias=g.epsc[:, 1:2],
          waits=(dve,), sem=act)
    sc.op("scalar", "activation", R(1), R(0), AF.Exp, scale=-0.5, sem=act,
          ser=True)
    sc.op("vector", "tensor_scalar", y_dst, x1[:], mu, R(1),
          OP.subtract, OP.mult, waits=((act, act.n),) + tuple(waits_y), sem=dve,
          ser=True)



def _prep_inputs(inputs):
    x = np.ascontiguousarray(inputs["L0_attn_inputs"])
    w_qkv = np.ascontiguousarray(inputs["L0_attn_W_qkv"])
    w_out = np.ascontiguousarray(inputs["L0_attn_W_out"])
    w1 = np.ascontiguousarray(inputs["L0_ffn_w1"])
    w2 = np.ascontiguousarray(inputs["L0_ffn_w2"])
    sl = np.ascontiguousarray(inputs["L0_ffn_slot_embeds"].reshape(NS, D))
    gx = np.ascontiguousarray(inputs["L0_ffn_gamma_x"].reshape(1, D))
    gs = np.ascontiguousarray(inputs["L0_ffn_gamma_slot"].reshape(8, 128).T)
    identf = np.eye(128, dtype=np.float32)
    identb = np.eye(128, dtype=ml_dtypes.bfloat16)
    trib = np.triu(np.ones((128, 128), dtype=np.float32)).astype(ml_dtypes.bfloat16)
    shared = dict(w_qkv=w_qkv, w_out=w_out, w1=w1, w2=w2, sl_emb=sl,
                  gx_row=gx, gs_t=gs, c_identf=identf, c_identb=identb,
                  c_trib=trib)
    return x, shared


def kernel(**inputs):
    x, shared = _prep_inputs(inputs)
    nc = build_program(nb=NB, debug=False)
    in_maps = []
    for c in range(8):
        m = dict(shared)
        m["x_in"] = np.ascontiguousarray(x[NB * c:NB * (c + 1)])
        in_maps.append(m)
    res = run_bass_kernel_spmd(nc, in_maps, core_ids=list(range(8)))
    return np.concatenate([r["y_out"] for r in res.results], axis=0)


# version 23
# speedup vs baseline: 1.0710x; 1.0018x over previous
import contextlib
import numpy as np
import ml_dtypes
import concourse.bass as bass
import concourse.mybir as mybir
from concourse.alu_op_type import AluOpType as OP
from concourse.bass_utils import run_bass_kernel_spmd

F32 = mybir.dt.float32
BF16 = mybir.dt.bfloat16
AF = mybir.ActivationFunctionType
AX = mybir.AxisListType

NB = 4
L = 2048
D = 1024
NDC = 8
NSEQC = 16
NQT = 4
H = 16
NHP = 8
NS = 4096
NSC = 32
HIDC = 32
BND_SCALE = -0.126


class Sem:
    def __init__(self, h):
        self.h = h
        self.n = 0

    def bump(self, k=1):
        self.n += k
        return self.n


class Script:
    ENGS = ("sync", "tensor", "vector", "scalar", "gpsimd")

    def __init__(self, nc):
        self.nc = nc
        self.q = {e: [] for e in self.ENGS}
        self._waited = {e: {} for e in self.ENGS}

    def _wait(self, eng, sem, val):
        if val is None or val <= 0:
            return
        w = self._waited[eng]
        if w.get(sem, -1) >= val:
            return
        w[sem] = val

        def f(e, h=sem.h, v=val):
            e.wait_ge(h, v)
        self.q[eng].append(f)

    def op(self, eng, method, *args, waits=(), sem=None, k=1, ser=False, **kw):
        if ser:
            own = self.eng2sem.get(eng)
            if own is not None:
                self._wait(eng, own, own.n)
        for wspec in waits:
            if isinstance(wspec, tuple):
                s, v = wspec
            else:
                s, v = wspec, wspec.n
            self._wait(eng, s, v)
        out_n = None
        if sem is not None:
            out_n = sem.bump(k)

        def f(e, method=method, args=args, kw=kw, sem=sem, k=k):
            r = getattr(e, method)(*args, **kw)
            if sem is not None:
                r.then_inc(sem.h, k)
        self.q[eng].append(f)
        return out_n

    def wait(self, eng, sem, val=None):
        self._wait(eng, sem, sem.n if val is None else val)

    def emit(self, name=None):
        with self.nc.Block(name) as blk:
            for eng in self.ENGS:
                fns = self.q[eng]
                if not fns:
                    continue
                getattr(blk, eng)(lambda e, fns=fns: [f(e) for f in fns])
        self.q = {e: [] for e in self.ENGS}
        self._waited = {e: {} for e in self.ENGS}


class Ctx:
    pass


def build_program(nb=NB, debug=False):
    nc = bass.Bass("TRN2", target_bir_lowering=False)
    g = Ctx()
    g.nc = nc
    g.nb = nb

    g.x_in = nc.dram_tensor("x_in", [nb, L, D], F32, kind="ExternalInput").ap()
    g.w_qkv = nc.dram_tensor("w_qkv", [D, 3 * D], F32, kind="ExternalInput").ap()
    g.w_out = nc.dram_tensor("w_out", [D, D], F32, kind="ExternalInput").ap()
    g.w1 = nc.dram_tensor("w1", [D, 4 * D], F32, kind="ExternalInput").ap()
    g.w2 = nc.dram_tensor("w2", [4 * D, D], F32, kind="ExternalInput").ap()
    g.sl_emb = nc.dram_tensor("sl_emb", [NS, D], F32, kind="ExternalInput").ap()
    g.gx_row = nc.dram_tensor("gx_row", [1, D], F32, kind="ExternalInput").ap()
    g.gs_t = nc.dram_tensor("gs_t", [128, 8], F32, kind="ExternalInput").ap()
    g.c_identf = nc.dram_tensor("c_identf", [128, 128], F32, kind="ExternalInput").ap()
    g.c_identb = nc.dram_tensor("c_identb", [128, 128], BF16, kind="ExternalInput").ap()
    g.c_trib = nc.dram_tensor("c_trib", [128, 128], BF16, kind="ExternalInput").ap()
    g.y_out = nc.dram_tensor("y_out", [nb, L, D], F32, kind="ExternalOutput").ap()

    dbg = "ExternalOutput" if debug else "Internal"
    g.se_hbm = nc.dram_tensor("se_hbm", [NSC, 128, 1024], BF16, kind=dbg).ap()
    g.z_hbm = nc.dram_tensor("z_hbm", [NSC, 128, L], F32, kind=dbg).ap()
    g.x2_hbm = nc.dram_tensor("x2_hbm", [NSEQC, 128, D], F32, kind=dbg).ap()
    g.sl_hbm = nc.dram_tensor("sl_hbm", [8, 128, 4096], BF16, kind=dbg).ap()
    g.p_hbm = nc.dram_tensor("p_hbm", [16, 128, 2048], BF16, kind=dbg).ap()

    with contextlib.ExitStack() as es:
        E = es.enter_context

        g.pe = Sem(E(nc.semaphore("pe")))
        g.act = Sem(E(nc.semaphore("act")))
        g.dve = Sem(E(nc.semaphore("dve")))
        g.gpe = Sem(E(nc.semaphore("gpe")))
        names_sync = ["cst", "seld0", "seld1", "sest0", "sest1", "x0", "x1",
                      "x2sp", "seb0", "seb1", "zsp0", "zsp1", "slsp0", "slsp1",
                      "slb0", "slb1", "psp0", "psp1", "pld", "pld1", "pld2", "pld3", "zs0", "zs1",
                      "zs2", "zs3", "x2l0", "x2l1", "yst0", "yst1"]
        names_gp = ["wq", "wk", "wv", "wo", "w1g", "w2g",
                    "w1a", "w1b", "w1c", "w1d", "w2a", "w2b", "w2c", "w2d"]
        g.S = {n2: Sem(E(nc.semaphore(n2))) for n2 in names_sync + names_gp}

        g.identf = E(nc.sbuf_tensor("identf", [128, 128], F32))
        g.identb = E(nc.sbuf_tensor("identb", [128, 128], BF16))
        g.trib = E(nc.sbuf_tensor("trib", [128, 128], BF16))
        g.ones_row_f = E(nc.sbuf_tensor("ones_row_f", [64, 128], F32))
        g.ones_col_b = E(nc.sbuf_tensor("ones_col_b", [128, 1], BF16))
        g.ones_row_b = E(nc.sbuf_tensor("ones_row_b", [64, 128], BF16))
        g.gammabc = E(nc.sbuf_tensor("gammabc", [128, D], F32))
        g.gst = E(nc.sbuf_tensor("gst", [128, 8], F32))
        g.junk = E(nc.sbuf_tensor("junk", [128, 1024], BF16))
        g.epsc = E(nc.sbuf_tensor("epsc", [128, 2], F32))
        g.rows = E(nc.sbuf_tensor("rows", [128, 64], F32))
        g.trow = E(nc.sbuf_tensor("trow", [64, 1024], F32))
        g.bias_col = E(nc.sbuf_tensor("bias_col", [128, 1], F32))

        g.PS_A = E(nc.psum_tensor("PS_A", [128, 2048], F32))
        g.PS_B = E(nc.psum_tensor("PS_B", [128, 1024], F32))

        sc = Script(nc)
        sc.eng2sem = {"tensor": g.pe, "vector": g.dve, "scalar": g.act,
                      "gpsimd": g.gpe}
        g.sc = sc

        _s0(g, es)

        for b in range(nb):
            with contextlib.ExitStack() as es_b:
                _batch_attn_p1(g, es_b, b)
                with contextlib.ExitStack() as es_p2:
                    _ffn_p2(g, es_p2, b)
                with contextlib.ExitStack() as es_p3:
                    _ffn_p3(g, es_p3, b)

        sc.wait("sync", g.S["yst0"])
        sc.wait("sync", g.S["yst1"])
        sc.emit("fin")

    return nc


def _s0(g, es):
    nc, sc = g.nc, g.sc
    pe, act, dve, gpe = g.pe, g.act, g.dve, g.gpe
    cst = g.S["cst"]

    sc.op("sync", "dma_start", g.identf[:], g.c_identf, sem=cst, k=16)
    sc.op("sync", "dma_start", g.identb[:], g.c_identb, sem=cst, k=16)
    sc.op("sync", "dma_start", g.trib[:], g.c_trib, sem=cst, k=16)
    sc.op("sync", "dma_start", g.gst[:], g.gs_t, sem=cst, k=16)
    sc.op("gpsimd", "memset", g.ones_row_f[:], 1.0, sem=gpe)
    sc.op("gpsimd", "memset", g.ones_col_b[:], 1.0, sem=gpe)
    sc.op("gpsimd", "memset", g.ones_row_b[:], 1.0, sem=gpe)
    sc.op("gpsimd", "memset", g.epsc[:, 0:1], 1e-30, sem=gpe)
    sc.op("gpsimd", "memset", g.epsc[:, 1:2], 1e-5, sem=gpe)
    with contextlib.ExitStack() as es00:
        gxr = es00.enter_context(nc.sbuf_tensor("gxr", [1, D], F32))
        sc.op("sync", "dma_start", gxr[:], g.gx_row, sem=cst, k=16)
        t_gx = cst.n
        for i in range(2):
            sc.op("tensor", "matmul", g.PS_B[:, 512 * i:512 * (i + 1)],
                  g.ones_row_f[0:1, :], gxr[0:1, 512 * i:512 * (i + 1)],
                  start=True, stop=True,
                  waits=((cst, t_gx), (gpe, 4)), sem=pe)
        sc.op("scalar", "copy", g.gammabc[:], g.PS_B[:, 0:1024], waits=(pe,), sem=act)
        sc.emit("s0_const")

    with contextlib.ExitStack() as es0:
        E0 = es0.enter_context
        PS_X = E0(nc.psum_tensor("PS_X0", [128, 1024], BF16))
        srows = E0(nc.sbuf_tensor("srows", [128, 16], F32))
        se_tm = E0(nc.sbuf_tensor("se_tm", [128, 2 * D], F32))
        se_bf = E0(nc.sbuf_tensor("se_bf", [128, 2 * D], BF16))
        se_st = E0(nc.sbuf_tensor("se_st", [128, 2 * D], BF16))
        seld = [g.S["seld0"], g.S["seld1"]]
        sest = [g.S["sest0"], g.S["sest1"]]
        t_rd = [0, 0]
        t_tr = [0, 0]
        t_bf_ok = [0, 0]
        t_rd_act = [0, 0]
        t_ps = [0]

        def s0_tail(c):
            r = c % 2
            bf = se_bf[:, r * D:(r + 1) * D]
            st = se_st[:, r * D:(r + 1) * D]
            for dc in range(8):
                sc.op("tensor", "transpose", PS_X[:, 128 * dc:128 * (dc + 1)],
                      bf[:, 128 * dc:128 * (dc + 1)], g.identb[:],
                      waits=((dve, t_bf_ok[r]), (act, t_ps[0])), sem=pe)
            t_tr[r] = pe.n
            for dc in range(8):
                sc.op("scalar", "activation", st[:, 128 * dc:128 * (dc + 1)],
                      PS_X[:, 128 * dc:128 * (dc + 1)], AF.Copy,
                      scale=g.gst[:, dc:dc + 1],
                      waits=((pe, t_tr[r]), (cst, t_gx), (sest[r], sest[r].n)),
                      sem=act)
            t_ps[0] = act.n
            sc.op("sync", "dma_start", g.se_hbm[c], st,
                  waits=(act,), sem=sest[r], k=16)

        for c in range(NSC):
            r = c % 2
            tm = se_tm[:, r * D:(r + 1) * D]
            bf = se_bf[:, r * D:(r + 1) * D]
            sc.op("sync", "dma_start", tm, g.sl_emb[128 * c:128 * (c + 1), :],
                  waits=((dve, t_rd[r]), (act, t_rd_act[r])), sem=seld[r], k=16)
            if c > 0:
                s0_tail(c - 1)
            RS = lambda i: srows[:, 8 * r + i:8 * r + i + 1]
            sc.op("scalar", "activation", g.junk[:, 0:D], tm, AF.Square,
                  accum_out=RS(0), waits=(seld[r],), sem=act)
            t_rd_act[r] = act.n
            sc.op("scalar", "activation", RS(1), RS(0), AF.Ln,
                  bias=g.epsc[:, 0:1], sem=act, ser=True)
            sc.op("scalar", "activation", RS(2), RS(1), AF.Exp,
                  scale=0.5, sem=act, ser=True)
            sc.op("vector", "tensor_scalar", RS(3), RS(2),
                  1e-12, None, OP.max, waits=(act,), sem=dve)
            sc.op("vector", "reciprocal", RS(4), RS(3), sem=dve,
                  ser=True)
            sc.op("vector", "tensor_scalar", RS(5), RS(4),
                  32.0, None, OP.mult, sem=dve, ser=True)
            sc.op("vector", "tensor_scalar", bf, tm, RS(5), None,
                  OP.mult, waits=((pe, t_tr[r]),), sem=dve, ser=True)
            t_bf_ok[r] = dve.n
            t_rd[r] = dve.n
        s0_tail(NSC - 1)
        for s2 in (seld[0], seld[1], sest[0], sest[1]):
            sc.wait("sync", s2)
        sc.emit("s0_se")


def _batch_attn_p1(g, es_b, b):
    nc, sc = g.nc, g.sc
    pe, act, dve, gpe = g.pe, g.act, g.dve, g.gpe
    PS_A, PS_B = g.PS_A, g.PS_B
    sx = [g.S["x0"], g.S["x1"]]
    es_qkv = contextlib.ExitStack()
    EB = es_qkv.enter_context
    PS_C = EB(nc.psum_tensor(f"PS_Ca_{b}", [128, 512], F32))

    q_fm = EB(nc.sbuf_tensor(f"q_fm_{b}", [128, NHP * L], BF16))
    k_fm = EB(nc.sbuf_tensor(f"k_fm_{b}", [128, NHP * L], BF16))
    v_tm = EB(nc.sbuf_tensor(f"v_tm_{b}", [128, NSEQC * 1040], BF16))
    x_tm = EB(nc.sbuf_tensor(f"x_tm_{b}", [128, 2 * D], F32))

    qf = lambda hp, s0, n: q_fm[:, hp * L + s0: hp * L + s0 + n]
    kf = lambda hp, s0, n: k_fm[:, hp * L + s0: hp * L + s0 + n]
    vt = lambda scn, c0, n: v_tm[:, scn * 1040 + c0: scn * 1040 + c0 + n]

    for eng in ("gpsimd", "sync"):
        sc.wait(eng, pe)
        sc.wait(eng, act)
        sc.wait(eng, dve)
    g.b_m = es_b.enter_context(
        nc.sbuf_tensor(f"b_m_{b}", [128, L], F32, side="right"))
    es_xn = contextlib.ExitStack()
    for scn in range(NSEQC):
        sc.op("gpsimd", "memset",
              v_tm.rearrange("p (s h c) -> p s h c", s=NSEQC, h=16)[:, scn, :, 64:65],
              1.0, sem=gpe)
    t_vones = gpe.n

    with contextlib.ExitStack() as es_a1:
        EA = es_a1.enter_context
        PS_Cb = EA(nc.psum_tensor(f"PS_Cb_{b}", [128, 512], F32))
        x_bf = EA(nc.sbuf_tensor(f"x_bf_{b}", [128, NDC * L], BF16))
        wbuf = EA(nc.sbuf_tensor(f"wbuf_{b}", [128, 2 * NDC * 1024], BF16))
        ksq = EA(nc.sbuf_tensor(f"ksq_{b}", [128, 2 * L], BF16))
        wv = lambda r, dc: wbuf[:, (r * NDC + dc) * 1024:(r * NDC + dc + 1) * 1024]
        xb = lambda dc, s0, n: x_bf[:, dc * L + s0: dc * L + s0 + n]

        t_free = [0, 0]
        t_psA = [0, 0]
        t_xbf_st = {}

        def x_load2(p2):
            sc.op("sync", "dma_start",
                  x_tm.rearrange("p (two d) -> p two d", two=2),
                  g.x_in[b, 256 * p2:256 * (p2 + 1), :].rearrange(
                      "(two p) d -> p two d", two=2),
                  waits=((pe, t_free[0]),), sem=sx[p2 % 2], k=16)

        x_load2(0)
        for scn in range(NSEQC):
            r = scn % 2
            pair = scn // 2
            xt = x_tm[:, r * D:(r + 1) * D]
            psx = PS_A[:, r * 1024:(r + 1) * 1024]
            for dc in range(NDC):
                sc.op("tensor", "transpose", psx[:, 128 * dc:128 * (dc + 1)],
                      xt[:, 128 * dc:128 * (dc + 1)], g.identf[:],
                      waits=((sx[pair % 2], sx[pair % 2].n), (dve, t_psA[r])),
                      sem=pe)
            if r == 1:
                t_free[0] = pe.n
                if pair + 1 < NSEQC // 2:
                    x_load2(pair + 1)
            sc.op("vector", "tensor_copy",
                  x_bf.rearrange("p (dc s) -> p dc s", dc=NDC)[
                      :, :, 128 * scn:128 * (scn + 1)],
                  psx.rearrange("p (dc c) -> p dc c", dc=NDC),
                  waits=(pe,), sem=dve)
            t_psA[r] = dve.n
            if scn % 4 == 3:
                t_xbf_st[scn // 4] = dve.n

        swq, swk, swv = g.S["wq"], g.S["wk"], g.S["wv"]
        for dc in range(NDC):
            sc.op("gpsimd", "dma_start", wv(0, dc),
                  g.w_qkv[128 * dc:128 * (dc + 1), 0:1024], sem=swq, k=16)
        for dc in range(NDC):
            sc.op("gpsimd", "dma_start", wv(1, dc),
                  g.w_qkv[128 * dc:128 * (dc + 1), 1024:2048], sem=swk, k=16)

        t_ev = [0, 0]
        for (name, ws, wr, dst) in (("q", swq, 0, qf), ("k", swk, 1, kf)):
            for st in range(4):
                for hp in range(NHP):
                    r = (hp + st * NHP) % 2
                    ps = PS_B[:, 512 * r:512 * (r + 1)]
                    for dc in range(NDC):
                        w = ((ws, ws.n), (act, t_ev[r]),
                             (dve, t_xbf_st[st])) if dc == 0 else ()
                        sc.op("tensor", "matmul", ps,
                              wv(wr, dc)[:, 128 * hp:128 * (hp + 1)],
                              xb(dc, 512 * st, 512),
                              start=(dc == 0), stop=(dc == NDC - 1),
                              waits=w, sem=pe)
                    sc.op("scalar", "copy", dst(hp, 512 * st, 512), ps,
                          waits=(pe,), sem=act)
                    t_ev[r] = act.n
            if name == "q":
                sc.wait("gpsimd", pe)
                for dc in range(NDC):
                    sc.op("gpsimd", "dma_start", wv(0, dc),
                          g.w_qkv[128 * dc:128 * (dc + 1), 2048:3072],
                          sem=swv, k=16)

        t_evv = [0, 0]
        for scn in range(NSEQC):
            for vf in range(2):
                r = (scn * 2 + vf) % 2
                ps = PS_B[:, 512 * r:512 * (r + 1)]
                for dc in range(NDC):
                    w = [(swv, swv.n)]
                    if dc == 0:
                        w += [(dve, t_evv[r]), (act, t_ev[r]), (gpe, t_vones)]
                    sc.op("tensor", "matmul", ps,
                          xb(dc, 128 * scn, 128),
                          wv(0, dc)[:, 512 * vf:512 * (vf + 1)],
                          start=(dc == 0), stop=(dc == NDC - 1),
                          waits=w if dc == 0 else (), sem=pe)
                sc.op("vector", "tensor_copy",
                      v_tm.rearrange("p (s h c) -> p s h c", s=NSEQC, h=16)[
                          :, scn, vf * 8:(vf + 1) * 8, 0:64],
                      ps.rearrange("p (h c) -> p h c", h=8),
                      waits=(pe,), sem=dve)
                t_evv[r] = dve.n

        t_ksq2 = [0, 0]
        t_psn_rd = [0, 0]
        idx = 0
        for ti, src in enumerate((qf, kf)):
            first = True
            rr = 0
            for hp in range(NHP):
                kk = ksq[:, (hp % 2) * L:(hp % 2 + 1) * L]
                sc.op("vector", "tensor_tensor", kk, src(hp, 0, L),
                      src(hp, 0, L), OP.mult,
                      waits=((pe, t_ksq2[hp % 2]), (act, t_ev[0]),
                             (act, t_ev[1])),
                      sem=dve)
                t_ksq_ok = dve.n
                for st in range(4):
                    idx2 = (hp * 4 + st) % 2
                    psn = PS_C if idx2 == 0 else PS_Cb
                    sc.op("tensor", "matmul", psn[0:1, 0:512],
                          g.ones_col_b[:], kk[:, 512 * st:512 * (st + 1)],
                          start=True, stop=True,
                          waits=((dve, t_ksq_ok), (dve, t_psn_rd[idx2])),
                          sem=pe)
                    t_ksq2[hp % 2] = pe.n
                    nrr = 1 - rr
                    if first:
                        sc.op("vector", "tensor_copy",
                              g.trow[32 * nrr:32 * nrr + 1, 0:512],
                              psn[0:1, 0:512], waits=(pe,), sem=dve, ser=True)
                        first = False
                    else:
                        sc.op("vector", "tensor_tensor",
                              g.trow[32 * nrr:32 * nrr + 1, 0:512],
                              psn[0:1, 0:512], g.trow[32 * rr:32 * rr + 1, 0:512],
                              OP.max, waits=(pe,), sem=dve, ser=True)
                    t_psn_rd[idx2] = dve.n
                    rr = nrr
            sc.op("vector", "tensor_reduce", g.rows[0:1, 8 + ti:9 + ti],
                  g.trow[32 * rr:32 * rr + 1, 0:512], AX.X, OP.max, sem=dve,
                  ser=True)
        sc.op("vector", "tensor_tensor", g.rows[0:1, 10:11], g.rows[0:1, 8:9],
              g.rows[0:1, 9:10], OP.mult, sem=dve, ser=True)
        sc.op("scalar", "activation", g.rows[0:1, 11:12], g.rows[0:1, 10:11],
              AF.Ln, bias=g.epsc[0:1, 0:1], waits=(dve,), sem=act)
        sc.op("scalar", "activation", g.rows[0:1, 12:13], g.rows[0:1, 11:12],
              AF.Exp, scale=0.5, sem=act, ser=True)
        sc.op("vector", "tensor_scalar", g.rows[0:1, 13:14], g.rows[0:1, 12:13],
              BND_SCALE, None, OP.mult, waits=(act,), sem=dve)
        sc.op("tensor", "matmul", PS_C[:, 0:1], g.ones_row_f[0:1, :],
              g.rows[0:1, 13:14], start=True, stop=True, waits=(dve,), sem=pe)
        sc.op("vector", "tensor_copy", g.bias_col[:], PS_C[:, 0:1],
              waits=(pe,), sem=dve)
        t_bias = dve.n
        sc.emit(f"b{b}_a15")

    with contextlib.ExitStack() as es_a2:
        EA = es_a2.enter_context
        PS_X = EA(nc.psum_tensor(f"PS_Xa_{b}", [128, 1024], BF16))
        wout = EA(nc.sbuf_tensor(f"wout_{b}", [128, NDC * 1024], BF16))
        attnout = EA(nc.sbuf_tensor(f"attnout_{b}", [128, NHP * 512], BF16))
        ebuf = EA(nc.sbuf_tensor(f"ebuf_{b}", [128, 2 * 2048], BF16))
        proj_fm = EA(nc.sbuf_tensor(f"proj_fm_{b}", [128, NDC * 512], BF16))
        x1b = EA(nc.sbuf_tensor(f"x1b_{b}", [128, D], F32))
        x2b = EA(nc.sbuf_tensor(f"x2b_{b}", [128, D], F32))
        bcnt = EA(nc.sbuf_tensor(f"bcnt_{b}", [128, 512], BF16))
        bcn = bcnt[:, :]
        avb = EA(nc.sbuf_tensor(f"avb_{b}", [128, 2 * 512], BF16))
        xn_tm = es_xn.enter_context(
            nc.sbuf_tensor(f"xn_tm_{b}", [128, NSEQC * D], BF16, side="right"))
        g.xn_tm = xn_tm

        swo = g.S["wo"]
        for fc in range(NDC):
            sc.op("gpsimd", "dma_start", wout[:, fc * 1024:(fc + 1) * 1024],
                  g.w_out[128 * fc:128 * (fc + 1), :], sem=swo, k=16)
        t_wo = swo.n

        ps_s2 = [PS_A[:, 0:1024], PS_A[:, 1024:2048]]
        ps_av2 = [PS_B[:, 0:512], PS_B[:, 512:1024]]
        t_exp = [0, 0]
        t_av_rd = [0, 0]
        t_avev = [0, 0]
        t_e_rd = {}
        t_bcn_rd = [0, 0]
        t_bcn_dve = [0, 0]
        t_projA = 0
        t_projB = 0
        t_psx_rd = 0
        t_xres = [0, 0]
        t_at_rd = 0

        def eslot(ha, gi):
            i = (gi % 2) * 2 + ha
            return ebuf[:, i * 1024:(i + 1) * 1024]

        def emit_scores(qt, hp, ha, gi):
            rbase = 64 * ha
            for ci in range(2):
                kc = 2 * gi + ci
                off = max(0, 128 * kc - 512 * qt)
                sc.op("tensor", "matmul",
                      ps_s2[ha][:, 512 * ci + off:512 * (ci + 1)],
                      kf(hp, 128 * kc, 128)[rbase:rbase + 64, :],
                      qf(hp, 512 * qt + off, 512 - off)[rbase:rbase + 64, :],
                      start=True, stop=True,
                      waits=((act, t_exp[ha]),) if ci == 0 else (), sem=pe)

        def emit_exp(qt, hp, ha, gi, t_s):
            eb = eslot(ha, gi)
            diag = (128 * (2 * gi) >= 512 * qt)
            ew = ((pe, t_s), (pe, t_e_rd.get((ha, gi % 2), 0)), (dve, t_bias))
            if not diag:
                sc.op("scalar", "activation", eb[:], ps_s2[ha][:],
                      AF.Exp, bias=g.bias_col[:, 0:1], scale=0.125,
                      waits=ew, sem=act)
            else:
                for ci in range(2):
                    kc = 2 * gi + ci
                    off = max(0, 128 * kc - 512 * qt)
                    sc.op("scalar", "activation",
                          eb[:, 512 * ci + off:512 * (ci + 1)],
                          ps_s2[ha][:, 512 * ci + off:512 * (ci + 1)],
                          AF.Exp, bias=g.bias_col[:, 0:1], scale=0.125,
                          waits=ew, sem=act)
                for ci in range(2):
                    kc = 2 * gi + ci
                    off = max(0, 128 * kc - 512 * qt)
                    sc.op("vector", "tensor_tensor",
                          eb[:, 512 * ci + off:512 * ci + off + 128],
                          eb[:, 512 * ci + off:512 * ci + off + 128],
                          g.trib[:], OP.mult, waits=(act,), sem=dve)
            t_exp[ha] = act.n

        def emit_av(qt, hp, ha, gi, G):
            hg = 2 * hp + ha
            eb = eslot(ha, gi)
            diag = (128 * (2 * gi) >= 512 * qt)
            for ci in range(2):
                kc = 2 * gi + ci
                off = max(0, 128 * kc - 512 * qt)
                w = [(act, t_exp[ha])]
                if diag:
                    w.append((dve, dve.n))
                if gi == 0 and ci == 0:
                    w += [(dve, t_avev[ha]),
                          (act, t_projA if ha == 0 else t_projB)]
                sc.op("tensor", "matmul",
                      ps_av2[ha][0:65, off:512],
                      vt(kc, 65 * hg, 65),
                      eb[:, 512 * ci + off:512 * (ci + 1)],
                      start=(gi == 0 and ci == 0),
                      stop=(gi == G - 1 and ci == 1),
                      waits=w, sem=pe)
            t_e_rd[(ha, gi % 2)] = pe.n

        pending = [None]

        def flush_finale():
            if pending[0] is None:
                return
            hp0, t_ln = pending[0]
            pending[0] = None
            slot = (hp0 % 2) * 512
            sl2 = avb[:, slot:slot + 512]
            for ha in range(2):
                sc.op("tensor", "matmul",
                      PS_C[64 * ha:64 * ha + 64, 0:512],
                      g.ones_row_b[32 * ha:32 * ha + 1, 0:64],
                      g.junk[32 * ha:32 * ha + 1,
                             512 * (hp0 % 2):512 * (hp0 % 2) + 512],
                      start=True, stop=True,
                      waits=((act, t_ln), (dve, t_bcn_dve[0]))
                      if ha == 0 else (), sem=pe)
            t_bcn_rd[0] = pe.n
            sc.op("vector", "tensor_copy", bcn, PS_C[0:128, 0:512],
                  waits=(pe,), sem=dve, ser=True)
            for ha in range(2):
                rbase = 64 * ha
                sc.op("vector", "tensor_tensor",
                      attnout[rbase:rbase + 64, hp0 * 512:(hp0 + 1) * 512],
                      sl2[rbase:rbase + 64, :], bcn[rbase:rbase + 64, :],
                      OP.mult, waits=((pe, t_at_rd),), sem=dve, ser=True)
                t_av_rd[ha] = dve.n
            t_bcn_dve[0] = dve.n

        for qt in range(NQT):
            G = 2 * (qt + 1)
            for hp in range(NHP):
                for gi in range(G):
                    emit_scores(qt, hp, 0, gi)
                    emit_scores(qt, hp, 1, gi)
                    t_s = pe.n
                    if gi == 0:
                        flush_finale()
                    if gi > 0:
                        emit_av(qt, hp, 0, gi - 1, G)
                        emit_av(qt, hp, 1, gi - 1, G)
                    diag = (128 * (2 * gi) >= 512 * qt)
                    par = gi % 2
                    eb2 = ebuf[:, par * 2048:(par + 1) * 2048]
                    ew = ((pe, t_s), (pe, t_e_rd.get((0, par), 0)),
                          (pe, t_e_rd.get((1, par), 0)), (dve, t_bias))
                    if not diag:
                        sc.op("scalar", "activation", eb2[:], PS_A[:],
                              AF.Exp, bias=g.bias_col[:, 0:1], scale=0.125,
                              waits=ew, sem=act)
                        t_exp[0] = act.n
                        t_exp[1] = act.n
                    else:
                        for ci in range(2):
                            kc = 2 * gi + ci
                            off = max(0, 128 * kc - 512 * qt)
                            n = 512 - off
                            sc.op("scalar", "activation",
                                  eb2.rearrange("p (h c) -> p h c", h=2)[
                                      :, :, 512 * ci + off:512 * (ci + 1)],
                                  PS_A.rearrange("p (h c) -> p h c", h=2)[
                                      :, :, 512 * ci + off:512 * (ci + 1)],
                                  AF.Exp, bias=g.bias_col[:, 0:1], scale=0.125,
                                  waits=ew, sem=act)
                        t_exp[0] = act.n
                        t_exp[1] = act.n
                        for ha2 in range(2):
                            for ci in range(2):
                                kc = 2 * gi + ci
                                off = max(0, 128 * kc - 512 * qt)
                                eb3 = eslot(ha2, gi)
                                sc.op("vector", "tensor_tensor",
                                      eb3[:, 512 * ci + off:512 * ci + off + 128],
                                      eb3[:, 512 * ci + off:512 * ci + off + 128],
                                      g.trib[:], OP.mult, waits=(act,), sem=dve)
                emit_av(qt, hp, 0, G - 1, G)
                emit_av(qt, hp, 1, G - 1, G)
                slot = (hp % 2) * 512
                sl2 = avb[:, slot:slot + 512]
                for ha in range(2):
                    sc.op("vector", "tensor_copy", sl2[64 * ha:64 * ha + 64, :],
                          ps_av2[ha][0:64, 0:512],
                          waits=((pe, pe.n), (dve, t_av_rd[ha])), sem=dve)
                    sc.op("vector", "tensor_copy",
                          g.trow[32 * ha:32 * ha + 1,
                                 512 * (hp % 2):512 * (hp % 2) + 512],
                          ps_av2[ha][64:65, 0:512],
                          waits=((pe, t_bcn_rd[0]),), sem=dve)
                    t_avev[ha] = dve.n
                for ha in range(2):
                    rrow = g.trow[32 * ha:32 * ha + 1,
                                  512 * (hp % 2):512 * (hp % 2) + 512]
                    sc.op("scalar", "activation", rrow, rrow, AF.Ln,
                          waits=((dve, t_avev[1]), (pe, t_bcn_rd[0])), sem=act)
                    sc.op("scalar", "activation",
                          g.junk[32 * ha:32 * ha + 1,
                                 512 * (hp % 2):512 * (hp % 2) + 512],
                          rrow, AF.Exp, scale=-1.0, sem=act, ser=True)
                pending[0] = (hp, act.n)
            flush_finale()
            t_heads = dve.n

            for dc in range(NDC):
                pr = dc % 2
                ps_p = PS_B[:, 512 * pr:512 * (pr + 1)]
                w = [(swo, t_wo), (dve, t_heads),
                     (act, t_projA if pr == 0 else t_projB)]
                for fc in range(NDC):
                    sc.op("tensor", "matmul", ps_p,
                          wout[:, fc * 1024 + 128 * dc:fc * 1024 + 128 * (dc + 1)],
                          attnout[:, fc * 512:(fc + 1) * 512],
                          start=(fc == 0), stop=(fc == NDC - 1),
                          waits=w if fc == 0 else (), sem=pe)
                t_at_rd = pe.n
                sc.op("scalar", "copy", proj_fm[:, dc * 512:(dc + 1) * 512], ps_p,
                      waits=(pe,), sem=act)
                if pr == 0:
                    t_projA = act.n
                else:
                    t_projB = act.n

            def xres_load(scl2):
                scn2 = 4 * qt + scl2
                r2 = scn2 % 2
                sc.op("sync", "dma_start", x_tm[:, r2 * D:(r2 + 1) * D],
                      g.x_in[b, 128 * scn2:128 * (scn2 + 1), :],
                      waits=((dve, t_xres[r2]),), sem=sx[r2], k=16)

            xres_load(0)
            xres_load(1)
            t_x1_free = getattr(g, "t_x1_free", 0)
            for scl in range(4):
                scn = 4 * qt + scl
                r = scn % 2
                xr = x_tm[:, r * D:(r + 1) * D]
                for dc in range(NDC):
                    sc.op("tensor", "transpose", PS_X[:, 128 * dc:128 * (dc + 1)],
                          proj_fm[:, dc * 512 + 128 * scl:dc * 512 + 128 * (scl + 1)],
                          g.identb[:],
                          waits=((act, max(t_projA, t_projB)), (dve, t_psx_rd))
                          if dc == 0 else (), sem=pe)
                sc.op("vector", "tensor_tensor", x1b[:], xr, PS_X[:],
                      OP.add, waits=((pe, pe.n), (sx[r], sx[r].n),
                                     (dve, t_x1_free)), sem=dve)
                t_psx_rd = dve.n
                t_xres[r] = dve.n
                if scl < 2:
                    xres_load(scl + 2)
                sx2 = g.S["x2sp"]
                _ln_rms(g, x1b, x2b[:],
                        xn_dst=xn_tm[:, scn * D:(scn + 1) * D],
                        waits_x2=((sx2, sx2.n),))
                sc.op("sync", "dma_start", g.x2_hbm[scn],
                      x2b[:], waits=(dve,), sem=sx2, k=16)
                t_x1_free = g.t_x1_free
        for s2 in (sx[0], sx[1], g.S["x2sp"]):
            sc.wait("sync", s2)
        for s2 in (swq, swk, swv, swo):
            sc.wait("gpsimd", s2)
        sc.emit(f"b{b}_a2")
    es_qkv.close()

    _ffn_p1(g, es_b, b)
    es_xn.close()


def _ln_rms(g, x1, x2_dst, xn_dst, waits_x2=()):
    sc, dve, act = g.sc, g.dve, g.act
    R = lambda i: g.rows[:, 16 + i:17 + i]
    sc.op("vector", "bn_stats", g.rows[:, 16:22], x1[:, 0:512], sem=dve, ser=True)
    sc.op("vector", "bn_stats", g.rows[:, 22:28], x1[:, 512:1024], sem=dve,
          ser=True)
    sc.op("vector", "bn_aggr", g.rows[:, 28:30], g.rows[:, 16:28], sem=dve,
          ser=True)
    mu, var = g.rows[:, 28:29], g.rows[:, 29:30]
    R = lambda i: g.rows[:, 30 + i:31 + i]
    sc.op("scalar", "activation", R(6), var, AF.Ln, bias=g.epsc[:, 1:2],
          waits=(dve,), sem=act)
    sc.op("scalar", "activation", R(7), R(6), AF.Exp, scale=-0.5, sem=act,
          ser=True)
    sc.op("scalar", "activation", R(8), var, AF.Ln, bias=g.epsc[:, 0:1], sem=act)
    sc.op("scalar", "activation", R(9), R(8), AF.Exp, scale=0.5, sem=act, ser=True)
    sc.op("vector", "tensor_scalar", x2_dst, x1[:], mu, R(7),
          OP.subtract, OP.mult, waits=((act, act.n),) + tuple(waits_x2), sem=dve,
          ser=True)
    g.t_x1_free = sc.eng2sem["vector"].n
    sc.op("vector", "tensor_tensor", R(10), R(9), R(7), OP.mult, sem=dve, ser=True)
    sc.op("vector", "tensor_scalar", R(11), R(10), 32.0, None, OP.mult, sem=dve,
          ser=True)
    sc.op("vector", "tensor_scalar", R(12), R(11), 1e-12, None, OP.max, sem=dve,
          ser=True)
    sc.op("vector", "reciprocal", R(13), R(12), sem=dve, ser=True)
    sc.op("vector", "tensor_scalar", R(14), R(13), 32.0, None, OP.mult, sem=dve,
          ser=True)
    sc.op("vector", "tensor_scalar", xn_dst, x2_dst, R(14), None, OP.mult,
          sem=dve, ser=True)
    sc.op("vector", "tensor_tensor", xn_dst, xn_dst, g.gammabc[:], OP.mult,
          sem=dve, ser=True)


def _ffn_p1(g, es_b, b):
    nc, sc = g.nc, g.sc
    pe, act, dve = g.pe, g.act, g.dve
    PS_A, PS_B = g.PS_A, g.PS_B
    xn_tm = g.xn_tm
    seb = [g.S["seb0"], g.S["seb1"]]
    zsp = [g.S["zsp0"], g.S["zsp1"]]
    slsp = [g.S["slsp0"], g.S["slsp1"]]

    with contextlib.ExitStack() as es:
        E = es.enter_context
        PS_X2 = E(nc.psum_tensor(f"PS_X2_{b}", [128, 2048], BF16))
        xn_fm = E(nc.sbuf_tensor(f"xn_fm_{b}", [128, NDC * L], BF16))
        cacc = E(nc.sbuf_tensor(f"cacc_{b}", [128, 2 * L], F32))
        zs = E(nc.sbuf_tensor(f"zs_{b}", [128, 2 * L], F32))
        du = E(nc.sbuf_tensor(f"du_{b}", [128, L], BF16))
        dbf = E(nc.sbuf_tensor(f"dbf_{b}", [128, 2 * L], BF16))
        dtm = E(nc.sbuf_tensor(f"dtm_{b}", [128, 2 * 16 * 512], BF16))
        sebuf = E(nc.sbuf_tensor(f"sebuf_{b}", [128, 2 * 1024], BF16))
        sl_st = E(nc.sbuf_tensor(f"sl_st_{b}", [128, 2 * 4096], BF16))
        xf = lambda dc, s0, n: xn_fm[:, dc * L + s0: dc * L + s0 + n]

        sc.wait("sync", pe)
        sc.wait("sync", act)
        sc.wait("sync", dve)
        t_psx = [0, 0]
        for scn in range(NSEQC):
            ph = scn % 2
            psx = PS_X2[:, ph * 1024:(ph + 1) * 1024]
            for dc in range(NDC):
                sc.op("tensor", "transpose", psx[:, 128 * dc:128 * (dc + 1)],
                      xn_tm[:, scn * D + 128 * dc:scn * D + 128 * (dc + 1)],
                      g.identb[:],
                      waits=((act, t_psx[ph]),) if dc == 0 else (), sem=pe)
            sc.op("scalar", "copy",
                  xn_fm.rearrange("p (dc s) -> p dc s", dc=NDC)[
                      :, :, 128 * scn:128 * (scn + 1)],
                  psx.rearrange("p (dc c) -> p dc c", dc=NDC),
                  waits=(pe,), sem=act)
            t_psx[ph] = act.n
        t_xnfm = act.n

        t_se = [0, 0]
        t_zps = [0, 0]
        t_zs_dve = [0, 0]
        t_zs_act = [0, 0]
        t_du_rd = 0
        t_dbf_rd = [0, 0]
        t_dtm_rd = [0, 0]
        t_dtm_wr = {}
        t_psx_rd = [0, 0]
        t_slots_ev = [0, 0]
        t_cacc = [0, 0]

        def emit_tr(j):
            hf = (j // 4) % 2
            jj = j % 4
            dbfs = dbf[:, (j % 2) * L:(j % 2 + 1) * L]
            for wv2 in range(2):
                ph = wv2 % 2
                psx = PS_X2[:, ph * 1024:(ph + 1) * 1024]
                for scw in range(8):
                    scn = wv2 * 8 + scw
                    sc.op("tensor", "transpose",
                          psx[:, 128 * scw:128 * (scw + 1)],
                          dbfs[:, 128 * scn:128 * (scn + 1)], g.identb[:],
                          waits=((dve, t_dbf_ok[j % 2]), (act, t_psx_rd[ph]),
                                 (pe, t_dtm_rd[hf]))
                          if scw == 0 else (), sem=pe)
                sc.op("scalar", "copy",
                      dtm.rearrange("p (h s jj c) -> p h s jj c",
                                    h=2, s=16, jj=4)[
                          :, hf, wv2 * 8:(wv2 + 1) * 8, jj],
                      psx.rearrange("p (s c) -> p s c", s=8),
                      waits=(pe,), sem=act)
                t_psx_rd[ph] = act.n
            t_dbf_rd[j % 2] = pe.n
            t_dtm_wr[j] = act.n

        def emit_slots(q, i):
            hf = q % 2
            half = i % 2
            for dc in (2 * i, 2 * i + 1):
                for scn in range(NSEQC):
                    w = []
                    if dc == 2 * i and scn == 0:
                        w = [(act, t_dtm_wr[4 * q + 3]),
                             (act, t_slots_ev[half])]
                    sc.op("tensor", "matmul",
                          PS_A[:, (dc % 4) * 512:(dc % 4) * 512 + 512],
                          xn_tm[:, scn * D + dc * 128:scn * D + (dc + 1) * 128],
                          dtm[:, (hf * 16 + scn) * 512:(hf * 16 + scn) * 512 + 512],
                          start=(scn == 0), stop=(scn == NSEQC - 1),
                          waits=w, sem=pe)
            if i == 3:
                t_dtm_rd[hf] = pe.n
            rs = q % 2
            sc.op("scalar", "copy",
                  sl_st[:, rs * 4096 + 1024 * i:rs * 4096 + 1024 * (i + 1)],
                  PS_A[:, (2 * i % 4) * 512:(2 * i % 4) * 512 + 1024],
                  waits=((pe, pe.n), (slsp[rs], slsp[rs].n)), sem=act)
            t_slots_ev[half] = act.n
            if i == 3:
                sc.op("sync", "dma_start", g.sl_hbm[q],
                      sl_st[:, rs * 4096:(rs + 1) * 4096],
                      waits=(act,), sem=slsp[rs], k=16)

        t_dbf_ok = [0, 0]

        def seb_load(j2):
            r2 = j2 % 2
            sc.op("sync", "dma_start", sebuf[:, r2 * 1024:(r2 + 1) * 1024],
                  g.se_hbm[j2],
                  waits=((pe, t_se[r2]),), sem=seb[r2], k=16)

        seb_load(0)
        for j in range(NSC):
            r = j % 2
            zsl = zs[:, r * L:(r + 1) * L]
            if j + 1 < NSC:
                seb_load(j + 1)
            for st in range(4):
                zr = st % 2
                ps = PS_B[:, 512 * zr:512 * (zr + 1)]
                for dc in range(NDC):
                    w = [(seb[r], seb[r].n)]
                    if dc == 0:
                        w += [(act, t_zps[zr]), (act, t_xnfm)]
                    sc.op("tensor", "matmul", ps,
                          sebuf[:, r * 1024 + dc * 128:r * 1024 + (dc + 1) * 128],
                          xf(dc, 512 * st, 512),
                          start=(dc == 0), stop=(dc == NDC - 1),
                          waits=w if dc == 0 else (), sem=pe)
                if st % 2 == 1:
                    sc.op("scalar", "copy", zsl[:, 512 * (st - 1):512 * (st + 1)],
                          PS_B[:, 0:1024],
                          waits=((pe, pe.n), (dve, t_zs_dve[r]),
                                 (act, t_zs_act[r]), (zsp[r], zsp[r].n)),
                          sem=act)
                    t_zps[0] = act.n
                    t_zps[1] = act.n
            t_se[r] = pe.n
            t_zrdy = act.n
            if j > 0:
                emit_tr(j - 1)
            if j >= 4:
                emit_slots(j // 4 - 1, j % 4)
            sc.op("sync", "dma_start", g.z_hbm[j], zsl,
                  waits=((act, t_zrdy),), sem=zsp[r], k=16)
            if j == 0:
                sc.op("vector", "tensor_copy", cacc[:, 0:L], zsl,
                      waits=((act, t_zrdy),), sem=dve)
            else:
                sc.op("vector", "tensor_tensor", cacc[:, r * L:(r + 1) * L],
                      zsl, cacc[:, (1 - r) * L:(2 - r) * L], OP.max,
                      waits=((act, t_zrdy), (dve, t_cacc[r])), sem=dve, ser=True)
            t_cacc[1 - r] = dve.n
            sc.op("vector", "tensor_reduce", g.rows[:, 32:33], zsl, AX.X,
                  OP.max, negate=True, waits=((act, t_zrdy),), sem=dve)
            t_zs_dve[r] = dve.n
            sc.op("scalar", "activation", du[:], zsl, AF.Exp,
                  bias=g.rows[:, 32:33], accum_out=g.rows[:, 33:34],
                  waits=((dve, dve.n), (dve, t_du_rd)), sem=act)
            t_zs_act[r] = act.n
            sc.op("vector", "reciprocal", g.rows[:, 34:35], g.rows[:, 33:34],
                  waits=(act,), sem=dve)
            sc.op("vector", "tensor_scalar", dbf[:, r * L:(r + 1) * L], du[:],
                  g.rows[:, 34:35], None, OP.mult,
                  waits=((pe, t_dbf_rd[r]),), sem=dve, ser=True)
            t_dbf_ok[r] = dve.n
            t_du_rd = dve.n
        emit_tr(NSC - 1)
        for i in range(4):
            emit_slots(NSC // 4 - 1, i)

        fin = cacc[:, ((NSC - 1) % 2) * L:(((NSC - 1) % 2) + 1) * L]
        for st16 in range(NSEQC):
            sc.op("tensor", "transpose", PS_B[:, 0:128],
                  fin[:, 128 * st16:128 * (st16 + 1)], g.identf[:],
                  waits=((act, act.n), (dve, dve.n)), sem=pe)
            sc.op("vector", "tensor_reduce", g.rows[:, 36:37], PS_B[:, 0:128],
                  AX.X, OP.max, waits=(pe,), sem=dve)
            sc.op("tensor", "transpose", PS_B[0:1, 128:256], g.rows[:, 36:37],
                  g.identf[:], waits=(dve,), sem=pe)
            sc.op("vector", "tensor_copy", g.trow[0:1, 0:128],
                  PS_B[0:1, 128:256], waits=(pe,), sem=dve)
            sc.op("tensor", "matmul", PS_A[:, 128 * st16:128 * (st16 + 1)],
                  g.ones_row_f[0:1, :], g.trow[0:1, 0:128],
                  start=(st16 % 4 == 0), stop=(st16 % 4 == 3),
                  waits=((dve, dve.n), (act, max(t_slots_ev))), sem=pe)
        sc.op("scalar", "copy", g.b_m[:], PS_A[:], waits=(pe,), sem=act)
        for s2 in (seb[0], seb[1], zsp[0], zsp[1], slsp[0], slsp[1]):
            sc.wait("sync", s2)
        sc.emit(f"b{b}_p1")


def _ffn_p2(g, es_p2, b):
    nc, sc = g.nc, g.sc
    pe, act, dve = g.pe, g.act, g.dve
    PS_A, PS_B = g.PS_A, g.PS_B
    slb_s = [g.S["slb0"], g.S["slb1"]]
    psp = [g.S["psp0"], g.S["psp1"]]
    E = es_p2.enter_context
    w1b = E(nc.sbuf_tensor(f"w1b_{b}", [128, NDC * 4096], BF16))
    w2b = E(nc.sbuf_tensor(f"w2b_{b}", [128, HIDC * 1024], BF16))
    slb = E(nc.sbuf_tensor(f"slb_{b}", [128, 2 * 4096], BF16))
    hb = E(nc.sbuf_tensor(f"hb_{b}", [128, 2 * 512], BF16))
    p_st = E(nc.sbuf_tensor(f"p_st_{b}", [128, 2 * 2048], BF16))

    for eng in ("gpsimd", "sync"):
        sc.wait(eng, pe)
        sc.wait(eng, act)
        sc.wait(eng, dve)
    w1s = [g.S["w1a"], g.S["w1b"], g.S["w1c"], g.S["w1d"]]
    w2s = [g.S["w2a"], g.S["w2b"], g.S["w2c"], g.S["w2d"]]
    for quarter in range(4):
        for dc in range(2 * quarter, 2 * quarter + 2):
            sc.op("gpsimd", "dma_start", w1b[:, dc * 4096:(dc + 1) * 4096],
                  g.w1[128 * dc:128 * (dc + 1), :], sem=w1s[quarter], k=16)
        for hc in range(8 * quarter, 8 * quarter + 8):
            sc.op("gpsimd", "dma_start", w2b[:, hc * 1024:(hc + 1) * 1024],
                  g.w2[128 * hc:128 * (hc + 1), :], sem=w2s[quarter], k=16)
    t_w1q = [s4.n for s4 in w1s]
    t_w2q = [s4.n for s4 in w2s]

    t_sl_rd = [0, 0]
    t_h_rd = [0, 0]
    t_hps = [0, 0]
    t_pev = 0
    sc.op("sync", "dma_start", slb[:, 0:4096], g.sl_hbm[0],
          waits=((pe, t_sl_rd[0]),), sem=slb_s[0], k=16)
    for jp in range(16):
        q = jp // 2
        r = q % 2
        sl = slb[:, r * 4096:(r + 1) * 4096]
        if jp % 2 == 0 and q + 1 < 8:
            rn = (q + 1) % 2
            sc.op("sync", "dma_start", slb[:, rn * 4096:(rn + 1) * 4096],
                  g.sl_hbm[q + 1],
                  waits=((pe, t_sl_rd[rn]),), sem=slb_s[rn], k=16)
        t_gelu = {}

        def p2_pmm(hc):
            base = ((hc // 2) % 2) * 512 + (hc % 2) * 256
            for jj in range(2):
                for dh in range(2):
                    w = []
                    if jj == 0 and dh == 0:
                        w = [(act, t_gelu[hc]), (w2s[hc // 8], t_w2q[hc // 8])]
                        if hc == 0:
                            w.append((act, t_pev))
                    sc.op("tensor", "matmul",
                          PS_A[:, jj * 1024 + dh * 512:jj * 1024 + (dh + 1) * 512],
                          hb[:, base + jj * 128:base + (jj + 1) * 128],
                          w2b[:, hc * 1024 + dh * 512:hc * 1024 + (dh + 1) * 512],
                          start=(hc == 0), stop=(hc == HIDC - 1),
                          waits=w, sem=pe)
            t_h_rd[(hc // 2) % 2] = pe.n

        for hc in range(HIDC):
            bk = (hc // 2) % 2
            ps_h = PS_B[:, 512 * bk + 256 * (hc % 2):512 * bk + 256 * (hc % 2) + 256]
            for dc in range(NDC):
                w = [(slb_s[r], slb_s[r].n), (w1s[dc // 2], t_w1q[dc // 2])]
                if dc == 0:
                    w += [(act, t_hps[bk])]
                sc.op("tensor", "matmul", ps_h,
                      w1b[:, dc * 4096 + 128 * hc:dc * 4096 + 128 * (hc + 1)],
                      sl[:, dc * 512 + (jp % 2) * 256:dc * 512 + (jp % 2) * 256 + 256],
                      start=(dc == 0), stop=(dc == NDC - 1),
                      waits=w, sem=pe)
            t_sl_rd[r] = pe.n
            if hc % 2 == 1:
                sc.op("scalar", "activation",
                      hb[:, bk * 512:(bk + 1) * 512],
                      PS_B[:, 512 * bk:512 * bk + 512],
                      AF.Gelu, waits=((pe, pe.n), (pe, t_h_rd[bk])), sem=act)
                t_hps[bk] = act.n
                t_gelu[hc - 1] = act.n
                t_gelu[hc] = act.n
            if hc > 1:
                p2_pmm(hc - 2)
        p2_pmm(HIDC - 2)
        p2_pmm(HIDC - 1)
        sc.op("scalar", "copy", p_st[:, r * 2048:(r + 1) * 2048], PS_A[:],
              waits=((pe, pe.n), (psp[r], psp[r].n)), sem=act)
        t_pev = act.n
        sc.op("sync", "dma_start", g.p_hbm[jp], p_st[:, r * 2048:(r + 1) * 2048],
              waits=(act,), sem=psp[r], k=16)
    for s2 in (slb_s[0], slb_s[1], psp[0], psp[1]):
        sc.wait("sync", s2)
    for s2 in w1s + w2s:
        sc.wait("gpsimd", s2)
    sc.emit(f"b{b}_p2")


def _ffn_p3(g, es_p3, b):
    nc, sc = g.nc, g.sc
    pe, act, dve = g.pe, g.act, g.dve
    PS_A, PS_B = g.PS_A, g.PS_B
    szs = [g.S["zs0"], g.S["zs1"], g.S["zs2"], g.S["zs3"]]
    sx2l = [g.S["x2l0"], g.S["x2l1"]]
    syst = [g.S["yst0"], g.S["yst1"]]
    E = es_p3.enter_context
    PS_C = E(nc.psum_tensor(f"PS_Cp_{b}", [128, 512], F32))
    PS_D = E(nc.psum_tensor(f"PS_Dp_{b}", [128, 256], F32))
    p_res = E(nc.sbuf_tensor(f"p_res_{b}", [128, 16 * 2048], BF16))
    zsl = E(nc.sbuf_tensor(f"zsl_{b}", [128, 4 * 256], F32))
    csub = E(nc.sbuf_tensor(f"csub_{b}", [128, 2 * 256], F32))
    cu = E(nc.sbuf_tensor(f"cu_{b}", [128, 2 * 256], BF16))
    bsr = E(nc.sbuf_tensor(f"bsr_{b}", [128, 256], F32))
    ffnraw = E(nc.sbuf_tensor(f"ffnraw_{b}", [128, 2 * 2048], F32))
    ffn_fm = E(nc.sbuf_tensor(f"ffn_fm_{b}", [128, 2048], F32))
    x2sl = E(nc.sbuf_tensor(f"x2sl_{b}", [128, 2 * D], F32))
    x1b = E(nc.sbuf_tensor(f"x1b_p3_{b}", [128, D], F32))
    yb = E(nc.sbuf_tensor(f"yb_{b}", [128, 2 * D], F32))

    sc.wait("sync", pe)
    sc.wait("sync", act)
    sc.wait("sync", dve)
    spl4 = [g.S["pld"], g.S["pld1"], g.S["pld2"], g.S["pld3"]]
    for jp in range(16):
        sc.op("sync", "dma_start", p_res[:, jp * 2048:(jp + 1) * 2048],
              g.p_hbm[jp], sem=spl4[jp // 4], k=16)
    t_p4 = [s4.n for s4 in spl4]
    pr = lambda j, dc: p_res[:, (j // 2) * 2048 + (j % 2) * 1024 + dc * 128:
                             (j // 2) * 2048 + (j % 2) * 1024 + (dc + 1) * 128]

    t_z_rd = [0] * 4
    t_cs_rd = [0, 0]
    t_cu_rd = [0, 0]
    t_psa_free = 0
    t_srow_dve = 0
    t_srow_ok = [0, 0]
    t_srow_act = [0, 0]
    t_raw_rd = [0, 0]
    t_mult2 = [0]
    t_bsr_rd = 0
    t_ps_t = 0
    t_x2 = [0, 0]
    t_y = [0, 0]
    t_ffn_rd = 0
    zdma = {}

    def emit_fin_a(stp):
        nonlocal t_bsr_rd, t_ffn_rd, t_ps_t
        rr = stp % 2
        srow = g.trow[0:1, 256 * rr:256 * rr + 256]
        jrow = g.junk[0:1, 256 * rr:256 * rr + 256]
        sc.op("scalar", "activation", srow, srow, AF.Ln,
              waits=((dve, t_srow_ok[rr]),), sem=act)
        sc.op("scalar", "activation", jrow, srow, AF.Exp, scale=-1.0,
              sem=act, ser=True)
        t_srow_act[rr] = act.n
        sc.op("tensor", "matmul", PS_D[:, 0:256], g.ones_row_b[0:1, :], jrow,
              start=True, stop=True,
              waits=((act, act.n), (dve, t_bsr_rd)), sem=pe)
        sc.op("vector", "tensor_copy", bsr[:, 0:256], PS_D[:, 0:256],
              waits=(pe,), sem=dve, ser=True)
        raw = ffnraw[:, rr * 2048:(rr + 1) * 2048]
        for dc in range(NDC):
            sc.op("vector", "tensor_tensor", ffn_fm[:, dc * 256:(dc + 1) * 256],
                  raw[:, dc * 256:(dc + 1) * 256], bsr[:, 0:256], OP.mult,
                  waits=((pe, t_ffn_rd),) if dc == 0 else (), sem=dve, ser=True)
        t_mult2[0] = dve.n
        t_raw_rd[rr] = dve.n
        t_bsr_rd = dve.n
        for sc2 in range(2):
            scn = stp * 2 + sc2
            r = scn % 2
            sc.op("sync", "dma_start", x2sl[:, r * D:(r + 1) * D],
                  g.x2_hbm[scn], waits=((dve, t_x2[r]),), sem=sx2l[r], k=16)

    def emit_fin_b(stp):
        nonlocal t_ffn_rd, t_ps_t
        for sc2 in range(2):
            scn = stp * 2 + sc2
            r = scn % 2
            for dc in range(NDC):
                w = []
                if dc == 0:
                    w = [(dve, t_mult2[0]), (dve, t_ps_t)]
                sc.op("tensor", "transpose", PS_B[:, 128 * dc:128 * (dc + 1)],
                      ffn_fm[:, dc * 256 + sc2 * 128:dc * 256 + sc2 * 128 + 128],
                      g.identf[:], waits=w, sem=pe)
            t_ffn_rd = pe.n
            sc.op("vector", "tensor_tensor", x1b[:], x2sl[:, r * D:(r + 1) * D],
                  PS_B[:], OP.add, waits=((pe, pe.n), (sx2l[r], sx2l[r].n)),
                  sem=dve, ser=True)
            t_ps_t = dve.n
            t_x2[r] = dve.n
            _ln_only(g, x1b, yb[:, r * D:(r + 1) * D],
                     waits_y=((syst[r], syst[r].n),))
            sc.op("sync", "dma_start", g.y_out[b, 128 * scn:128 * (scn + 1), :],
                  yb[:, r * D:(r + 1) * D], waits=(dve,), sem=syst[r], k=16)
            t_y[r] = syst[r].n
        return

    def _zload(st, j):
        r4 = j % 4
        sc.op("sync", "dma_start", zsl[:, r4 * 256:(r4 + 1) * 256],
              g.z_hbm[j][:, 256 * st:256 * (st + 1)],
              waits=((dve, t_z_rd[r4]),), sem=szs[r4], k=16)
        zdma[(st, j)] = (szs[r4], szs[r4].n)

    _zload(0, 0)
    _zload(0, 1)
    _zload(0, 2)
    for st in range(8):
        for j in range(NSC):
            r4 = j % 4
            nst, nj = st, j + 3
            if nj >= NSC:
                nst, nj = st + 1, nj - NSC
            if nst < 8:
                _zload(nst, nj)
            r2 = j % 2
            sc.op("vector", "tensor_tensor", csub[:, r2 * 256:(r2 + 1) * 256],
                  zsl[:, r4 * 256:(r4 + 1) * 256],
                  g.b_m[:, 256 * st:256 * (st + 1)], OP.subtract,
                  waits=(zdma[(st, j)], (act, t_cs_rd[r2])), sem=dve)
            t_z_rd[r4] = dve.n
            sc.op("scalar", "activation", cu[:, r2 * 256:(r2 + 1) * 256],
                  csub[:, r2 * 256:(r2 + 1) * 256], AF.Exp,
                  waits=((dve, dve.n), (pe, t_cu_rd[r2])), sem=act)
            t_cs_rd[r2] = act.n
            for dc in range(NDC):
                w = []
                if dc == 0:
                    w = [(act, act.n), (spl4[j // 8], t_p4[j // 8])]
                    if j == 0:
                        w += [(act, t_psa_free)]
                sc.op("tensor", "matmul", PS_A[:, dc * 256:(dc + 1) * 256],
                      pr(j, dc), cu[:, r2 * 256:(r2 + 1) * 256],
                      start=(j == 0 and dc % 2 == 0),
                      stop=(j == NSC - 1 and dc % 2 == 1),
                      waits=w, sem=pe)
            sc.op("tensor", "matmul", PS_C[0:1, 0:256], g.ones_col_b[:],
                  cu[:, r2 * 256:(r2 + 1) * 256],
                  start=(j == 0), stop=(j == NSC - 1),
                  waits=((dve, t_srow_dve),) if j == 0 else (),
                  sem=pe)
            t_cu_rd[r2] = pe.n
            if j == 2 and st > 0:
                emit_fin_a(st - 1)
            if j == 4 and st > 0:
                emit_fin_b(st - 1)
        rr = st % 2
        sc.op("vector", "tensor_copy", g.trow[0:1, 256 * rr:256 * rr + 256],
              PS_C[0:1, 0:256],
              waits=((pe, pe.n), (act, t_srow_act[rr])), sem=dve)
        t_srow_dve = dve.n
        t_srow_ok[rr] = dve.n
        sc.op("scalar", "copy", ffnraw[:, rr * 2048:(rr + 1) * 2048], PS_A[:],
              waits=((pe, pe.n), (dve, t_raw_rd[rr])), sem=act)
        t_psa_free = act.n
    emit_fin_a(7)
    emit_fin_b(7)
    for s2 in (spl4[0], spl4[1], spl4[2], spl4[3], szs[0], szs[1], szs[2],
               szs[3], sx2l[0], sx2l[1], syst[0], syst[1]):
        sc.wait("sync", s2)
    sc.emit(f"b{b}_p3")


def _ln_only(g, x1, y_dst, waits_y=()):
    sc, dve, act = g.sc, g.dve, g.act
    sc.op("vector", "bn_stats", g.rows[:, 48:54], x1[:, 0:512], sem=dve, ser=True)
    sc.op("vector", "bn_stats", g.rows[:, 54:60], x1[:, 512:1024], sem=dve,
          ser=True)
    sc.op("vector", "bn_aggr", g.rows[:, 60:62], g.rows[:, 48:60], sem=dve,
          ser=True)
    mu, var = g.rows[:, 60:61], g.rows[:, 61:62]
    R = lambda i: g.rows[:, 62 + i:63 + i]
    sc.op("scalar", "activation", R(0), var, AF.Ln, bias=g.epsc[:, 1:2],
          waits=(dve,), sem=act)
    sc.op("scalar", "activation", R(1), R(0), AF.Exp, scale=-0.5, sem=act,
          ser=True)
    sc.op("vector", "tensor_scalar", y_dst, x1[:], mu, R(1),
          OP.subtract, OP.mult, waits=((act, act.n),) + tuple(waits_y), sem=dve,
          ser=True)



def _prep_inputs(inputs):
    x = np.ascontiguousarray(inputs["L0_attn_inputs"])
    w_qkv = np.ascontiguousarray(inputs["L0_attn_W_qkv"])
    w_out = np.ascontiguousarray(inputs["L0_attn_W_out"])
    w1 = np.ascontiguousarray(inputs["L0_ffn_w1"])
    w2 = np.ascontiguousarray(inputs["L0_ffn_w2"])
    sl = np.ascontiguousarray(inputs["L0_ffn_slot_embeds"].reshape(NS, D))
    gx = np.ascontiguousarray(inputs["L0_ffn_gamma_x"].reshape(1, D))
    gs = np.ascontiguousarray(inputs["L0_ffn_gamma_slot"].reshape(8, 128).T)
    identf = np.eye(128, dtype=np.float32)
    identb = np.eye(128, dtype=ml_dtypes.bfloat16)
    trib = np.triu(np.ones((128, 128), dtype=np.float32)).astype(ml_dtypes.bfloat16)
    shared = dict(w_qkv=w_qkv, w_out=w_out, w1=w1, w2=w2, sl_emb=sl,
                  gx_row=gx, gs_t=gs, c_identf=identf, c_identb=identb,
                  c_trib=trib)
    return x, shared


def kernel(**inputs):
    x, shared = _prep_inputs(inputs)
    nc = build_program(nb=NB, debug=False)
    in_maps = []
    for c in range(8):
        m = dict(shared)
        m["x_in"] = np.ascontiguousarray(x[NB * c:NB * (c + 1)])
        in_maps.append(m)
    res = run_bass_kernel_spmd(nc, in_maps, core_ids=list(range(8)))
    return np.concatenate([r["y_out"] for r in res.results], axis=0)
